# version 28
# speedup vs baseline: 1.0189x; 1.0004x over previous
import contextlib

import numpy as np
import ml_dtypes
import concourse.bass as bass
import concourse.mybir as mybir
from concourse.bass_utils import run_bass_kernel_spmd

BF16 = mybir.dt.bfloat16
F32 = mybir.dt.float32
AF = mybir.ActivationFunctionType
ALU = mybir.AluOpType
AX = mybir.AxisListType.X
NBF = ml_dtypes.bfloat16

NCORES = 8
L10_PIPE = 2
T = 2048
D = 1024
HID = 1536


def f32(a):
    return np.ascontiguousarray(np.asarray(a), dtype=np.float32)


def nbf(a):
    return np.ascontiguousarray(np.asarray(a).astype(NBF))


class Sem:
    def __init__(self, nc, stack, name):
        self.h = stack.enter_context(nc.semaphore(name))
        self.n = 0

    def inc(self, instr, k=1):
        instr.then_inc(self.h, k)
        self.n += k
        return self.n


class B:
    def __init__(self, name=""):
        self.nc = bass.Bass("TRN2", target_bir_lowering=False)
        self.stack = contextlib.ExitStack()
        self.scopes = [self.stack]
        self.sems = {}
        self.pfx = ""

    def push(self, tag):
        self.scopes.append(contextlib.ExitStack())
        self.pfx = tag + "_"

    def pop(self):
        self.scopes.pop().close()
        self.pfx = ""

    def sem(self, name):
        if name not in self.sems:
            self.sems[name] = Sem(self.nc, self.stack, name)
        return self.sems[name]

    def sbuf(self, name, shape, dtype):
        return self.scopes[-1].enter_context(
            self.nc.sbuf_tensor(self.pfx + name, list(shape), dtype))

    def psum(self, name, shape, dtype=F32):
        return self.scopes[-1].enter_context(
            self.nc.psum_tensor(self.pfx + name, list(shape), dtype))

    def inp(self, name, shape, dtype=F32):
        return self.nc.dram_tensor(name, list(shape), dtype, kind="ExternalInput").ap()

    def out(self, name, shape, dtype=F32):
        return self.nc.dram_tensor(name, list(shape), dtype, kind="ExternalOutput").ap()

    def barrier(self):
        self.nc.all_engine_barrier()

    def prep_consts(self, values):
        vals = sorted(set(float(v) for v in values))
        cb = self.sbuf("cconsts", [128, len(vals)], F32)
        for i, v in enumerate(vals):
            self.nc.vector.memset(cb[:, i:i + 1], v)
            self.nc.const_aps.register(F32, v, cb[:, i:i + 1])

    def close(self):
        self.stack.close()

    def load(self, sbuf_ap, dram_ap, sem=None, eng=None):
        eng = eng or self.nc.sync
        s = sem or self.sem("ld")
        s.inc(eng.dma_start(out=sbuf_ap, in_=dram_ap), 16)
        return s.n

    def store(self, dram_ap, sbuf_ap, sem=None, eng=None):
        eng = eng or self.nc.gpsimd
        eng.nop(cycle_cnt=700)
        s = sem or self.sem("st")
        s.inc(eng.dma_start(out=dram_ap, in_=sbuf_ap), 16)
        return s.n


def col_tiles(total, w=512):
    out = []
    c = 0
    while c < total:
        out.append((c, min(w, total - c)))
        c += w
    return out


def col_tiles_ov(total, w=512):
    assert total >= w
    out = []
    c = 0
    while c + w <= total:
        out.append((c, w))
        c += w
    if c < total:
        out.append((total - w, w))
    return out


def load_weight_bf16(b, wt, dram_ap, n_ic, outw, stage=None, kparts=None, tag=""):
    nc = b.nc
    ld = b.sem("wld" + tag)
    row = 0
    for ic in range(n_ic):
        kp = 128 if kparts is None else kparts[ic]
        b.load(wt[0:kp, ic * outw:(ic + 1) * outw], dram_ap[row:row + kp, :],
               sem=ld)
        row += kp
    nc.vector.wait_ge(ld.h, ld.n)


def emit_proj(b, wt, outw, n_ic, n_oc, rhs_fn, evac_fn, tiles, banks,
              kparts=None, oc_m=128, tag=""):
    nc = b.nc
    mm = b.sem("mm" + tag)
    ev = b.sem("ev" + tag)
    ev_base = ev.n
    k = 0
    for (c0, w) in tiles:
        for oc in range(n_oc):
            ps = banks[k % len(banks)]
            if k >= len(banks):
                nc.tensor.wait_ge(ev.h, ev_base + k - len(banks) + 1)
            ins = None
            for ic in range(n_ic):
                kp = 128 if kparts is None else kparts[ic]
                wblk = wt(ic, oc) if callable(wt) else \
                    wt[0:kp, ic * outw + oc * 128: ic * outw + oc * 128 + oc_m]
                ins = nc.tensor.matmul(ps[0:oc_m, 0:w], wblk, rhs_fn(ic, c0, w),
                                       start=(ic == 0), stop=(ic == n_ic - 1))
            mm.inc(ins)
            ev_before = ev.n
            evac_fn(oc, c0, w, ps, mm, ev)
            assert ev.n == ev_before + 1, "evac must inc ev exactly once"
            k += 1


def emit_stats(b, chunk_fn, n_chunks, total, ones_col, sq_scratch, psrow,
               sum_row, sq_row, tag="", want_sum=True, want_sq=True,
               chunk_kp=None, tiles=None, chunk_src=None, ones_sq=None,
               bf_scratch=None, ones_bf=None):
    nc = b.nc
    sq = b.sem("sq" + tag)
    smm = b.sem("smm" + tag)
    sev = b.sem("sev" + tag)
    sqbuf = 0
    sev_base = sev.n
    tt2 = 0
    for (c0, w) in (tiles if tiles is not None else col_tiles(total)):
        for (dst, use_sq) in ((sum_row, False), (sq_row, True)):
            if (use_sq and not want_sq) or (not use_sq and not want_sum):
                continue
            ping = psrow.shape[-1] >= 1024
            hp = (tt2 % 2) * 512 if ping else 0
            if ping and tt2 >= 2:
                nc.tensor.wait_ge(sev.h, sev_base + tt2 - 1)
            ins = None
            for c in range(n_chunks):
                kp = 128 if chunk_kp is None else chunk_kp[c]
                src = (chunk_src(c, c0, w) if chunk_src is not None
                       else chunk_fn(c)[0:kp, c0:c0 + w])
                ones_use = ones_col
                if bf_scratch is not None:
                    half = (sqbuf % 2) * 512
                    if sqbuf >= 2:
                        nc.scalar.wait_ge(smm.h, smm.n)
                    sq.inc(nc.scalar.activation(
                        out=bf_scratch[0:kp, half:half + w], in_=src,
                        func=(AF.Square if use_sq else AF.Copy)))
                    nc.tensor.wait_ge(sq.h, sq.n)
                    src = bf_scratch[0:kp, half:half + w]
                    sqbuf += 1
                    ones_use = ones_bf
                elif use_sq:
                    half = (sqbuf % 2) * 512
                    if sqbuf >= 2:
                        nc.scalar.wait_ge(smm.h, smm.n)
                    sq.inc(nc.scalar.activation(
                        out=sq_scratch[0:kp, half:half + w], in_=src,
                        func=AF.Square))
                    nc.tensor.wait_ge(sq.h, sq.n)
                    src = sq_scratch[0:kp, half:half + w]
                    sqbuf += 1
                    if ones_sq is not None:
                        ones_use = ones_sq
                ins = nc.tensor.matmul(psrow[0:1, hp:hp + w],
                                       ones_use[0:kp, :], src,
                                       start=(c == 0), stop=(c == n_chunks - 1))
                smm.inc(ins)
            nc.vector.wait_ge(smm.h, smm.n)
            sev.inc(nc.vector.tensor_copy(out=dst[0:1, c0:c0 + w],
                                          in_=psrow[0:1, hp:hp + w]))
            if not ping:
                nc.tensor.wait_ge(sev.h, sev.n)
            tt2 += 1
    nc.tensor.wait_ge(sev.h, sev.n)


def emit_bcast(b, psdst, row_ap, w, ones_row, tag=""):
    nc = b.nc
    s = b.sem("bc" + tag)
    s.inc(nc.tensor.matmul(psdst[:, 0:w], ones_row[0:1, :], row_ap,
                           start=True, stop=True))
    return s


def emit_rstd(b, sum_row, sq_row, total, inv_n, eps, rstd_row, mu_row, scr,
              tag="", want_mu=True, rstd_bf=None, mu_bf=None):
    nc = b.nc
    v = nc.vector
    s = nc.scalar
    sm = b.sem("rst" + tag)
    if want_mu:
        sm.inc(v.tensor_scalar_mul(out=mu_row[0:1, 0:total],
                                   in0=sum_row[0:1, 0:total], scalar1=inv_n))
    sm.inc(v.tensor_scalar_mul(out=scr[0:1, 0:total], in0=sq_row[0:1, 0:total],
                               scalar1=inv_n))
    if want_mu:
        sm.inc(v.tensor_mul(out=rstd_row[0:1, 0:total], in0=mu_row[0:1, 0:total],
                            in1=mu_row[0:1, 0:total]))
        sm.inc(v.tensor_sub(out=scr[0:1, 0:total], in0=scr[0:1, 0:total],
                            in1=rstd_row[0:1, 0:total]))
    s.wait_ge(sm.h, sm.n)
    sm.inc(s.activation(out=scr[0:1, 0:total], in_=scr[0:1, 0:total],
                        func=AF.Sqrt, bias=eps))
    v.wait_ge(sm.h, sm.n)
    sm.inc(v.reciprocal(out=rstd_row[0:1, 0:total], in_=scr[0:1, 0:total]))
    if rstd_bf is not None:
        s.wait_ge(sm.h, sm.n)
        sm.inc(s.activation(out=rstd_bf[0:1, 0:total],
                            in_=rstd_row[0:1, 0:total], func=AF.Copy))
        if want_mu and mu_bf is not None:
            sm.inc(s.activation(out=mu_bf[0:1, 0:total],
                                in_=mu_row[0:1, 0:total], func=AF.Copy))
    return sm


def emit_L01(b, A):
    W = T + 4
    nc = b.nc
    x0g = A["x0g"]
    n1w = A["n1w"]
    pre1o = A["pre1"]
    n1o = A["n1"]

    xg = b.sbuf("xg", [128, 8 * W], F32)
    pre1 = b.sbuf("pre1s", [128, 8 * W], BF16)
    n1 = b.sbuf("n1s", [128, 8 * W], BF16)
    wcol = b.sbuf("wcol", [128, 8], F32)
    ones_col = b.sbuf("ones_col", [128, 1], F32)
    ones_row = b.sbuf("ones_row", [1, 128], F32)
    onesrb = b.sbuf("onesrb", [1, 128], BF16)
    ones_b = b.sbuf("ones_b", [128, 1], BF16)
    sq = b.sbuf("sq", [128, 1024], F32)
    sqb = b.sbuf("sqb", [128, 1024], BF16)
    scr = b.sbuf("scr", [128, 512], F32)
    srow = b.sbuf("srow", [1, W], F32)
    qrow = b.sbuf("qrow", [1, W], F32)
    rrow = b.sbuf("rrow", [1, W], F32)
    mrow = b.sbuf("mrow", [1, W], F32)
    scrow = b.sbuf("scrow", [1, W], F32)
    psrow = b.psum("psrow", [1, 1024])
    psb1 = b.psum("psb1", [128, 512])
    psb2 = b.psum("psb2", [128, 512])

    ld = b.sem("ld")
    for c in range(8):
        b.load(xg[:, c * W:(c + 1) * W], x0g[c * 128:(c + 1) * 128, :], sem=ld)
    b.load(wcol[:], n1w, sem=ld)
    nc.vector.wait_ge(ld.h, ld.n)
    nc.vector.memset(ones_col[:], 1.0)
    nc.vector.memset(ones_row[:], 1.0)
    nc.vector.memset(onesrb[:], 1.0)
    nc.vector.memset(ones_b[:], 1.0)
    b.prep_consts([1e-5])
    b.barrier()

    emit_stats(b, lambda c: xg[:, c * W:(c + 1) * W], 8, W, ones_col, sq,
               psrow, srow, qrow, tag="a", tiles=col_tiles_ov(W),
               bf_scratch=sqb, ones_bf=ones_b)
    b.barrier()
    rbw_339 = b.sbuf("rbw339", [1, W], BF16)
    mbw_339 = b.sbuf("mbw339", [1, W], BF16)
    emit_rstd(b, srow, qrow, W, 1.0 / D, 1e-5, rrow, mrow, scrow, tag="a", rstd_bf=rbw_339, mu_bf=mbw_339)
    b.barrier()
    ap = b.sem("ap")
    for (c0, w) in col_tiles_ov(W):
        emit_bcast(b, psb1, rbw_339[0:1, c0:c0 + w], w, onesrb, tag="a")
        bc = emit_bcast(b, psb2, mbw_339[0:1, c0:c0 + w], w, onesrb, tag="a")
        nc.vector.wait_ge(bc.h, bc.n)
        for c in range(8):
            g = c * W + c0
            ap.inc(nc.vector.tensor_sub(out=scr[:, 0:w], in0=xg[:, g:g + w],
                                        in1=psb2[:, 0:w]))
            ap.inc(nc.vector.tensor_mul(out=pre1[:, g:g + w], in0=scr[:, 0:w],
                                        in1=psb1[:, 0:w]))
        nc.tensor.wait_ge(ap.h, ap.n)
    b.barrier()

    emit_stats(b, lambda c: pre1[:, c * W:(c + 1) * W], 8, W, ones_col, sqb,
               psrow, srow, qrow, tag="b", want_sum=False,
               tiles=col_tiles_ov(W), ones_sq=ones_b)
    b.barrier()
    rbw_359 = b.sbuf("rbw359", [1, W], BF16)
    emit_rstd(b, srow, qrow, W, 1.0 / D, 1e-5, rrow, mrow, scrow, tag="b",
              want_mu=False, rstd_bf=rbw_359)
    b.barrier()
    ap2 = b.sem("ap2")
    for (c0, w) in col_tiles_ov(W):
        bc = emit_bcast(b, psb1, rbw_359[0:1, c0:c0 + w], w, onesrb, tag="b")
        nc.vector.wait_ge(bc.h, bc.n)
        for c in range(8):
            g = c * W + c0
            ap2.inc(nc.vector.scalar_tensor_tensor(
                out=n1[:, g:g + w], in0=pre1[:, g:g + w],
                scalar=wcol[:, c:c + 1], in1=psb1[:, 0:w],
                op0=ALU.mult, op1=ALU.mult))
        nc.tensor.wait_ge(ap2.h, ap2.n)
    b.barrier()

    st = b.sem("st")
    for c in range(8):
        b.store(pre1o[c * 128:(c + 1) * 128, :], pre1[:, c * W:(c + 1) * W], sem=st)
        b.store(n1o[c * 128:(c + 1) * 128, :], n1[:, c * W:(c + 1) * W], sem=st)
    nc.gpsimd.wait_ge(st.h, st.n)
    return


def _dram_store_evac(b, dst_dram, ev_tiles, func=AF.Gelu, bias=None,
                    out_stride=None):
    nc = b.nc
    st = b.sem("st")

    def evac(oc, c0, w, ps, mm, ev):
        i = ev.n % len(ev_tiles)
        nc.scalar.wait_ge(mm.h, mm.n)
        nb = len(ev_tiles)
        if st.n >= 16 * nb:
            nc.scalar.wait_ge(st.h, st.n - 16 * (nb - 1))
        kw = {}
        if bias is not None:
            kw["bias"] = bias[:, oc:oc + 1]
        ev.inc(nc.scalar.activation(out=ev_tiles[i][:, 0:w], in_=ps[:, 0:w],
                                    func=func, **kw))
        nc.gpsimd.wait_ge(ev.h, ev.n)
        b.store(dst_dram[oc * 128:(oc + 1) * 128, c0:c0 + w],
                ev_tiles[i][:, 0:w], sem=st)
    return evac


def emit_L02(b, A):
    nc = b.nc
    n1i = A["n1"]
    wg = A["wgt"]
    ggo = A["gg"]

    n1 = b.sbuf("n1s", [128, 8 * T], BF16)
    wt = b.sbuf("wt", [128, 8 * HID], BF16)
    ev_t = [b.sbuf(f"ev{i}", [128, 512], BF16) for i in range(4)]
    banks = [b.psum(f"pb{i}", [128, 512]) for i in range(4)]

    ld = b.sem("ld")
    for c in range(8):
        b.load(n1[:, c * T:(c + 1) * T], n1i[c * 128:(c + 1) * 128, :], sem=ld)
    load_weight_bf16(b, wt, wg, 8, HID)
    nc.vector.wait_ge(ld.h, ld.n)
    b.barrier()
    emit_proj(b, wt, HID, 8, 12,
              lambda ic, c0, w: n1[:, ic * T + c0:ic * T + c0 + w],
              _dram_store_evac(b, ggo, ev_t, AF.Gelu), col_tiles(T), banks)
    nc.gpsimd.wait_ge(b.sem("st").h, b.sem("st").n)
    return


def emit_L03(b, A):
    W = T + 4
    nc = b.nc
    n1i = A["n1"]
    wi = A["wit"]
    cw = A["convw"]
    cb = A["convb"]
    uo = A["ucb"]

    n1 = b.sbuf("n1s", [128, 8 * W], BF16)
    wt = b.sbuf("wt", [128, 8 * HID], BF16)
    u0 = b.sbuf("u0", [128, 12 * W], BF16)
    cwv = b.sbuf("cwv", [128, 12 * 4], F32)
    cbv = b.sbuf("cbv", [128, 12], F32)
    t1 = b.sbuf("t1", [128, 1024], F32)
    t2 = b.sbuf("t2", [128, 1024], F32)
    outt = [b.sbuf(f"ot{i}", [128, 1024], BF16) for i in range(2)]
    banks = [b.psum(f"pb{i}", [128, 512]) for i in range(4)]

    ld = b.sem("ld")
    for c in range(8):
        b.load(n1[:, c * W:(c + 1) * W], n1i[c * 128:(c + 1) * 128, :], sem=ld)
    b.load(cwv[:], cw, sem=ld)
    b.load(cbv[:], cb, sem=ld)
    load_weight_bf16(b, wt, wi, 8, HID)
    nc.vector.wait_ge(ld.h, ld.n)
    b.barrier()

    def evac(oc, c0, w, ps, mm, ev):
        nc.vector.wait_ge(mm.h, mm.n)
        ev.inc(nc.vector.tensor_copy(out=u0[:, oc * W + c0:oc * W + c0 + w],
                                     in_=ps[:, 0:w]))

    emit_proj(b, wt, HID, 8, 12,
              lambda ic, c0, w: n1[:, ic * W + c0:ic * W + c0 + w],
              evac, col_tiles_ov(W), banks)
    b.barrier()

    cs = b.sem("cs")
    st = b.sem("st")
    for hc in range(12):
        for ti, (c0, w) in enumerate(col_tiles(T, 1024)):
            base = hc * W + c0
            ot = outt[(hc * 2 + ti) % 2]
            cs.inc(nc.vector.tensor_scalar(
                out=t1[:, 0:w], in0=u0[:, base + 1:base + 1 + w],
                scalar1=cwv[:, hc * 4:hc * 4 + 1], scalar2=None, op0=ALU.mult))
            cs.inc(nc.vector.scalar_tensor_tensor(
                out=t2[:, 0:w], in0=u0[:, base + 2:base + 2 + w],
                scalar=cwv[:, hc * 4 + 1:hc * 4 + 2], in1=t1[:, 0:w],
                op0=ALU.mult, op1=ALU.add))
            cs.inc(nc.vector.scalar_tensor_tensor(
                out=t1[:, 0:w], in0=u0[:, base + 3:base + 3 + w],
                scalar=cwv[:, hc * 4 + 2:hc * 4 + 3], in1=t2[:, 0:w],
                op0=ALU.mult, op1=ALU.add))
            cs.inc(nc.vector.scalar_tensor_tensor(
                out=t2[:, 0:w], in0=u0[:, base + 4:base + 4 + w],
                scalar=cwv[:, hc * 4 + 3:hc * 4 + 4], in1=t1[:, 0:w],
                op0=ALU.mult, op1=ALU.add))
            if st.n >= 32:
                nc.vector.wait_ge(st.h, st.n - 16)
            cs.inc(nc.vector.tensor_scalar(
                out=ot[:, 0:w], in0=t2[:, 0:w],
                scalar1=cbv[:, hc:hc + 1], scalar2=None, op0=ALU.add))
            nc.gpsimd.wait_ge(cs.h, cs.n)
            b.store(uo[hc * 128:(hc + 1) * 128, c0:c0 + w], ot[:, 0:w], sem=st)
    nc.gpsimd.wait_ge(st.h, st.n)
    return


def emit_L45(b, A):
    nc = b.nc
    ui = A["ucb"]
    w1 = A["rgin_t"]
    w2 = A["rggate_t"]
    b1 = A["rginb"]
    b2 = A["rggateb"]
    so = A["sig"]
    ro = A["srg"]

    u = b.sbuf("us", [128, 12 * T], BF16)
    wt1 = b.sbuf("wt1", [128, 12 * HID], BF16)
    wt2 = b.sbuf("wt2", [128, 12 * HID], BF16)
    b1v = b.sbuf("b1v", [128, 12], F32)
    b2v = b.sbuf("b2v", [128, 12], F32)
    ev_t = [b.sbuf(f"ev{i}", [128, 512], BF16) for i in range(4)]
    banks = [b.psum(f"pb{i}", [128, 512]) for i in range(4)]

    ld = b.sem("ld")
    for c in range(12):
        b.load(u[:, c * T:(c + 1) * T], ui[c * 128:(c + 1) * 128, :], sem=ld)
    b.load(b1v[:], b1, sem=ld)
    b.load(b2v[:], b2, sem=ld)
    load_weight_bf16(b, wt1, w1, 12, HID, tag="1")
    nc.vector.wait_ge(ld.h, ld.n)
    b.barrier()
    load_weight_bf16(b, wt2, w2, 12, HID, tag="2")

    rhs = lambda ic, c0, w: u[:, ic * T + c0:ic * T + c0 + w]
    emit_proj(b, wt1, HID, 12, 12, rhs,
              _dram_store_evac(b, so, ev_t, AF.Sigmoid, bias=b1v),
              col_tiles(T), banks)
    b.barrier()
    emit_proj(b, wt2, HID, 12, 12, rhs,
              _dram_store_evac(b, ro, ev_t, AF.Sigmoid, bias=b2v),
              col_tiles(T), banks)
    nc.gpsimd.wait_ge(b.sem("st").h, b.sem("st").n)
    return


def emit_L06(b, A):
    nc = b.nc
    ui = A["ucb"]
    si = A["sig"]
    ri = A["srg"]
    lam = A["lam"]
    hlo = A["hl"]
    po = A["pp"]
    pko = A["pack"]

    TT = 512
    ntiles = T // TT
    ub = [b.sbuf(f"ub{i}", [128, 12 * TT], BF16) for i in range(2)]
    sb = [b.sbuf(f"sb{i}", [128, 12 * TT], BF16) for i in range(2)]
    rb = [b.sbuf(f"rb{i}", [128, 12 * TT], BF16) for i in range(2)]
    lv = b.sbuf("lv", [128, 12], F32)
    c8 = b.sbuf("c8", [128, 12], F32)
    tmp12 = b.sbuf("tmp12", [128, 12], F32)
    S1 = b.sbuf("S1", [128, 12 * TT], F32)
    S2 = b.sbuf("S2", [128, 12 * TT], F32)
    S3 = b.sbuf("S3", [128, 12 * TT], F32)
    S4 = b.sbuf("S4", [128, 12 * TT], F32)
    zero = b.sbuf("zero", [128, TT], F32)
    hstate = b.sbuf("hstate", [128, 12], F32)
    pstate = b.sbuf("pstate", [128, 12], F32)
    hbf = b.sbuf("hbf", [128, 12 * TT], BF16)
    pbf = b.sbuf("pbf", [128, 12 * TT], BF16)
    pk = b.sbuf("pk", [128, 24], F32)

    ld = b.sem("ld")

    def load_tile(ti):
        i = ti % 2
        c0 = ti * TT
        for c in range(12):
            b.load(ub[i][:, c * TT:(c + 1) * TT],
                   ui[c * 128:(c + 1) * 128, c0:c0 + TT], sem=ld)
            b.load(sb[i][:, c * TT:(c + 1) * TT],
                   si[c * 128:(c + 1) * 128, c0:c0 + TT], sem=ld)
            b.load(rb[i][:, c * TT:(c + 1) * TT],
                   ri[c * 128:(c + 1) * 128, c0:c0 + TT], sem=ld)
        return ld.n

    b.load(lv[:], lam, sem=ld)
    ld_marks = [load_tile(0), load_tile(1)]
    nc.vector.memset(zero[:], 0.0)
    b.prep_consts([-1.0, 1.0, 1.0 + 1e-6])
    b.barrier()
    nc.scalar.wait_ge(ld.h, 16)
    cs = b.sem("cs")
    cs.inc(nc.scalar.activation(out=c8[:], in_=lv[:], func=AF.Abs))
    nc.scalar.wait_ge(cs.h, cs.n)
    cs.inc(nc.scalar.activation(out=c8[:], in_=c8[:], func=AF.Exp, scale=-1.0))
    nc.scalar.wait_ge(cs.h, cs.n)
    cs.inc(nc.scalar.activation(out=c8[:], in_=c8[:], func=AF.Ln, bias=1.0))
    nc.scalar.wait_ge(cs.h, cs.n)
    cs.inc(nc.scalar.activation(out=tmp12[:], in_=lv[:], func=AF.Relu))
    nc.vector.wait_ge(cs.h, cs.n)
    cs.inc(nc.vector.tensor_add(out=c8[:], in0=c8[:], in1=tmp12[:]))
    nc.vector.wait_ge(cs.h, cs.n)
    cs.inc(nc.vector.tensor_scalar_mul(out=c8[:], in0=c8[:], scalar1=-8.0))
    nc.scalar.wait_ge(cs.h, cs.n)
    nc.vector.wait_ge(cs.h, cs.n)
    b.barrier()

    st = b.sem("st")
    ph = b.sem("ph")
    for ti in range(ntiles):
        i = ti % 2
        nc.scalar.wait_ge(ld.h, ld_marks[ti])
        nc.scalar.wait_ge(ph.h, ph.n)
        for hc in range(12):
            s = hc * TT
            ph.inc(nc.scalar.activation(out=S1[:, s:s + TT],
                                        in_=rb[i][:, s:s + TT],
                                        func=AF.Exp, scale=c8[:, hc:hc + 1]))
        nc.vector.wait_ge(ph.h, ph.n)
        if ti > 0:
            nc.vector.wait_ge(st.h, st.n)
        for hc in range(12):
            s = hc * TT
            ph.inc(nc.vector.tensor_mul(out=S2[:, s:s + TT], in0=S1[:, s:s + TT],
                                        in1=S1[:, s:s + TT]))
        nc.scalar.wait_ge(ph.h, ph.n)
        for hc in range(12):
            s = hc * TT
            ph.inc(nc.scalar.activation(out=S3[:, s:s + TT], in_=S2[:, s:s + TT],
                                        func=AF.Sqrt, scale=-1.0, bias=1.0 + 1e-6))
        nc.vector.wait_ge(ph.h, ph.n)
        for hc in range(12):
            s = hc * TT
            ph.inc(nc.vector.tensor_mul(out=S2[:, s:s + TT], in0=sb[i][:, s:s + TT],
                                        in1=ub[i][:, s:s + TT]))
            ph.inc(nc.vector.tensor_mul(out=S4[:, s:s + TT], in0=S2[:, s:s + TT],
                                        in1=S3[:, s:s + TT]))
            ph.inc(nc.vector.tensor_tensor_scan(
                out=S3[:, s:s + TT], data0=S1[:, s:s + TT], data1=S4[:, s:s + TT],
                initial=(0.0 if ti == 0 else hstate[:, hc:hc + 1]),
                op0=ALU.mult, op1=ALU.add))
            ph.inc(nc.vector.tensor_copy(out=hbf[:, s:s + TT], in_=S3[:, s:s + TT]))
            ph.inc(nc.vector.tensor_copy(out=hstate[:, hc:hc + 1],
                                         in_=S3[:, s + TT - 1:s + TT]))
            ph.inc(nc.vector.tensor_tensor_scan(
                out=S4[:, s:s + TT], data0=S1[:, s:s + TT], data1=zero[:, 0:TT],
                initial=(1.0 if ti == 0 else pstate[:, hc:hc + 1]),
                op0=ALU.mult, op1=ALU.add))
            ph.inc(nc.vector.tensor_copy(out=pbf[:, s:s + TT], in_=S4[:, s:s + TT]))
            ph.inc(nc.vector.tensor_copy(out=pstate[:, hc:hc + 1],
                                         in_=S4[:, s + TT - 1:s + TT]))
        if ti + 2 < ntiles:
            nc.sync.wait_ge(ph.h, ph.n)
            ld_marks.append(load_tile(ti + 2))
        nc.gpsimd.wait_ge(ph.h, ph.n)
        c0 = ti * TT
        for hc in range(12):
            b.store(hlo[hc * 128:(hc + 1) * 128, c0:c0 + TT],
                    hbf[:, hc * TT:hc * TT + TT], sem=st)
            b.store(po[hc * 128:(hc + 1) * 128, c0:c0 + TT],
                    pbf[:, hc * TT:hc * TT + TT], sem=st)
    b.barrier()
    pkc = b.sem("pkc")
    pkc.inc(nc.vector.tensor_copy(out=pk[:, 0:12], in_=pstate[:]))
    pkc.inc(nc.vector.tensor_copy(out=pk[:, 12:24], in_=hstate[:]))
    nc.gpsimd.wait_ge(pkc.h, pkc.n)
    b.store(pko, pk[:], sem=st)
    nc.gpsimd.wait_ge(st.h, st.n)
    return


def emit_L07(b, A):
    nc = b.nc
    hli = A["hl"]
    ppi = A["pp"]
    ggi = A["gg"]
    pr1 = A["pre1"]
    pks = A["packs"]
    cmk = A["cmask"]
    wo = A["wot"]
    n2w = A["n2w"]
    bxo = A["bx"]
    n2o = A["n2"]

    hlt = [b.sbuf(f"hlt{i}", [128, T], BF16) for i in range(2)]
    ppt = [b.sbuf(f"ppt{i}", [128, T], BF16) for i in range(2)]
    ggt = [b.sbuf(f"ggt{i}", [128, T], BF16) for i in range(2)]
    mx = b.sbuf("mx", [128, 12 * T], BF16)
    pre1 = b.sbuf("pre1s", [128, 8 * T], BF16)
    wt = b.sbuf("wt", [128, 12 * D], BF16)
    pkv = b.sbuf("pkv", [128, 8 * 24], F32)
    cmv = b.sbuf("cmv", [128, 16], F32)
    H1 = b.sbuf("H1", [128, 12], F32)
    H2 = b.sbuf("H2", [128, 12], F32)
    ta = b.sbuf("ta", [128, 12], F32)
    tb = b.sbuf("tb", [128, 12], F32)
    bx = b.sbuf("bxs", [128, 8 * T], BF16)
    n2t = [b.sbuf(f"n2t{i}", [128, 512], BF16) for i in range(2)]
    wcol = b.sbuf("wcol", [128, 8], F32)
    ones_col = b.sbuf("ones_col", [128, 1], F32)
    ones_row = b.sbuf("ones_row", [1, 128], F32)
    onesrb = b.sbuf("onesrb", [1, 128], BF16)
    sq = b.sbuf("sq", [128, 1024], F32)
    evt = [b.sbuf(f"evt{i}", [128, 512], BF16) for i in range(2)]
    qrow = b.sbuf("qrow", [1, T], F32)
    rrow = b.sbuf("rrow", [1, T], F32)
    scrow = b.sbuf("scrow", [1, T], F32)
    psrow = b.psum("psrow", [1, 1024])
    psb1 = b.psum("psb1", [128, 512])
    banks = [b.psum(f"pb{i}", [128, 512]) for i in range(4)]

    ld = b.sem("ld")
    ldc = b.sem("ldc")

    def load_chunk(hc):
        i = hc % 2
        b.load(hlt[i][:], hli[hc * 128:(hc + 1) * 128, :], sem=ldc)
        b.load(ppt[i][:], ppi[hc * 128:(hc + 1) * 128, :], sem=ldc)
        b.load(ggt[i][:], ggi[hc * 128:(hc + 1) * 128, :], sem=ldc)
        return ldc.n

    for c in range(8):
        b.load(pre1[:, c * T:(c + 1) * T], pr1[c * 128:(c + 1) * 128, :], sem=ld)
    for r in range(NCORES):
        b.load(pkv[:, r * 24:(r + 1) * 24], pks[r * 128:(r + 1) * 128, :], sem=ld)
    b.load(cmv[:], cmk, sem=ld)
    b.load(wcol[:], n2w, sem=ld)
    load_weight_bf16(b, wt, wo, 12, D)
    nc.vector.wait_ge(ld.h, ld.n)
    nc.vector.memset(ones_col[:], 1.0)
    nc.vector.memset(ones_row[:], 1.0)
    nc.vector.memset(onesrb[:], 1.0)
    b.prep_consts([1e-5])
    fx = b.sem("fx")
    fx.inc(nc.vector.memset(H1[:], 0.0))
    nc.vector.wait_ge(fx.h, fx.n)
    cur, nxt = H1, H2
    for j in range(NCORES - 1):
        fx.inc(nc.vector.tensor_scalar(
            out=ta[:], in0=pkv[:, j * 24:j * 24 + 12],
            scalar1=cmv[:, j:j + 1], scalar2=cmv[:, 8 + j:8 + j + 1],
            op0=ALU.mult, op1=ALU.add))
        fx.inc(nc.vector.tensor_scalar(
            out=tb[:], in0=pkv[:, j * 24 + 12:j * 24 + 24],
            scalar1=cmv[:, j:j + 1], scalar2=None, op0=ALU.mult))
        nc.vector.wait_ge(fx.h, fx.n)
        fx.inc(nc.vector.tensor_mul(out=nxt[:], in0=ta[:], in1=cur[:]))
        nc.vector.wait_ge(fx.h, fx.n)
        fx.inc(nc.vector.tensor_add(out=nxt[:], in0=nxt[:], in1=tb[:]))
        nc.vector.wait_ge(fx.h, fx.n)
        cur, nxt = nxt, cur
    H = cur
    mxs = b.sem("mxs")
    marks = [load_chunk(0), load_chunk(1)]
    for hc in range(12):
        i = hc % 2
        g = hc * T
        nc.vector.wait_ge(ldc.h, marks[hc])
        mxs.inc(nc.vector.scalar_tensor_tensor(
            out=mx[:, g:g + T], in0=ppt[i][:], scalar=H[:, hc:hc + 1],
            in1=hlt[i][:], op0=ALU.mult, op1=ALU.add))
        mxs.inc(nc.vector.tensor_mul(out=mx[:, g:g + T], in0=mx[:, g:g + T],
                                     in1=ggt[i][:]))
        if hc + 2 < 12:
            nc.sync.wait_ge(mxs.h, mxs.n)
            marks.append(load_chunk(hc + 2))
    b.barrier()

    def evac(oc, c0, w, ps, mm, ev):
        i = ev.n % 2
        nc.scalar.wait_ge(mm.h, mm.n)
        nc.scalar.wait_ge(b.sem("add").h, b.sem("add").n)
        b.sem("cp").inc(nc.scalar.activation(out=evt[i][:, 0:w], in_=ps[:, 0:w],
                                             func=AF.Identity))
        nc.vector.wait_ge(b.sem("cp").h, b.sem("cp").n)
        ev.inc(nc.vector.tensor_add(
            out=bx[:, oc * T + c0:oc * T + c0 + w],
            in0=pre1[:, oc * T + c0:oc * T + c0 + w], in1=evt[i][:, 0:w]))
        b.sem("add").inc(nc.vector.engine_nop())

    emit_proj(b, wt, D, 12, 8,
              lambda ic, c0, w: mx[:, ic * T + c0:ic * T + c0 + w],
              evac, col_tiles(T), banks)
    b.barrier()
    sqb = b.sbuf("sqb", [128, 1024], BF16)
    ones_b = b.sbuf("ones_b", [128, 1], BF16)
    nc.vector.memset(ones_b[:], 1.0)
    b.barrier()
    emit_stats(b, lambda c: bx[:, c * T:(c + 1) * T], 8, T, ones_col, sqb,
               psrow, scrow, qrow, tag="r", want_sum=False, ones_sq=ones_b)
    b.barrier()
    rbw_802 = b.sbuf("rbw802", [1, T], BF16)
    emit_rstd(b, scrow, qrow, T, 1.0 / D, 1e-5, rrow, scrow, scrow, tag="r",
              want_mu=False, rstd_bf=rbw_802)
    b.barrier()
    ap = b.sem("ap")
    st = b.sem("st")
    for (c0, w) in col_tiles(T):
        bc = emit_bcast(b, psb1, rbw_802[0:1, c0:c0 + w], w, onesrb, tag="r")
        nc.vector.wait_ge(bc.h, bc.n)
        for c in range(8):
            g = c * T + c0
            i = c % 2
            if st.n >= 32:
                nc.vector.wait_ge(st.h, st.n - 16)
            ap.inc(nc.vector.scalar_tensor_tensor(
                out=n2t[i][:, 0:w], in0=bx[:, g:g + w],
                scalar=wcol[:, c:c + 1], in1=psb1[:, 0:w],
                op0=ALU.mult, op1=ALU.mult))
            nc.gpsimd.wait_ge(ap.h, ap.n)
            b.store(n2o[c * 128:(c + 1) * 128, c0:c0 + w], n2t[i][:, 0:w], sem=st)
        nc.tensor.wait_ge(ap.h, ap.n)
    b.barrier()
    st2 = b.sem("st2")
    for c in range(8):
        b.store(bxo[c * 128:(c + 1) * 128, :], bx[:, c * T:(c + 1) * T], sem=st2)
    nc.gpsimd.wait_ge(st2.h, st2.n)
    return


def emit_L08(b, A):
    nc = b.nc
    n2i = A["n2"]
    wg = A["mgt"]
    wu = A["mut"]
    to = A["tt"]

    n2 = b.sbuf("n2s", [128, 8 * T], BF16)
    wtg = b.sbuf("wtg", [128, 8 * HID], BF16)
    wtu = b.sbuf("wtu", [128, 8 * HID], BF16)
    gm = b.sbuf("gm", [128, 12 * T], BF16)
    ev_t = [b.sbuf(f"ev{i}", [128, 512], BF16) for i in range(4)]
    banks = [b.psum(f"pb{i}", [128, 512]) for i in range(4)]

    ld = b.sem("ld")
    for c in range(8):
        b.load(n2[:, c * T:(c + 1) * T], n2i[c * 128:(c + 1) * 128, :], sem=ld)
    load_weight_bf16(b, wtg, wg, 8, HID, tag="g")
    nc.vector.wait_ge(ld.h, ld.n)
    b.barrier()
    load_weight_bf16(b, wtu, wu, 8, HID, tag="u")

    rhs = lambda ic, c0, w: n2[:, ic * T + c0:ic * T + c0 + w]

    def evac_g(oc, c0, w, ps, mm, ev):
        nc.scalar.wait_ge(mm.h, mm.n)
        ev.inc(nc.scalar.activation(
            out=gm[:, oc * T + c0:oc * T + c0 + w], in_=ps[:, 0:w], func=AF.Gelu))

    emit_proj(b, wtg, HID, 8, 12, rhs, evac_g, col_tiles(T), banks, tag="g")
    b.barrier()
    st = b.sem("st")

    def evac_u(oc, c0, w, ps, mm, ev):
        i = ev.n % 4
        cp = b.sem("cpu")
        nc.scalar.wait_ge(mm.h, mm.n)
        if st.n >= 16 * 4:
            nc.scalar.wait_ge(st.h, st.n - 16 * 3)
        cp.inc(nc.scalar.activation(out=ev_t[i][:, 0:w], in_=ps[:, 0:w],
                                    func=AF.Identity))
        nc.vector.wait_ge(cp.h, cp.n)
        ev.inc(nc.vector.tensor_mul(
            out=ev_t[i][:, 0:w], in0=gm[:, oc * T + c0:oc * T + c0 + w],
            in1=ev_t[i][:, 0:w]))
        nc.gpsimd.wait_ge(ev.h, ev.n)
        b.store(to[oc * 128:(oc + 1) * 128, c0:c0 + w], ev_t[i][:, 0:w], sem=st)

    emit_proj(b, wtu, HID, 8, 12, rhs, evac_u, col_tiles(T), banks, tag="u")
    nc.gpsimd.wait_ge(st.h, st.n)
    return


def emit_L09(b, A):
    nc = b.nc
    ti = A["tt"]
    bxi = A["bx"]
    x0i = A["x0"]
    wd = A["mdt"]
    x1o = A["x1"]

    tt = b.sbuf("tts", [128, 12 * T], BF16)
    bx = b.sbuf("bxs", [128, 8 * T], BF16)
    wt = b.sbuf("wt", [128, 12 * D], BF16)
    a1 = b.sbuf("a1", [128, 8 * T], F32)
    evt = [b.sbuf(f"evt{i}", [128, 512], BF16) for i in range(2)]
    x0t = [b.sbuf(f"x0t{i}", [128, 512], F32) for i in range(2)]
    x1t = [b.sbuf(f"x1t{i}", [128, 512], F32) for i in range(2)]
    ones_col = b.sbuf("ones_col", [128, 1], F32)
    ones_row = b.sbuf("ones_row", [1, 128], F32)
    onesrb = b.sbuf("onesrb", [1, 128], BF16)
    sq = b.sbuf("sq", [128, 512], F32)
    scr = sq[:, 0:512]
    srow = b.sbuf("srow", [1, T], F32)
    qrow = b.sbuf("qrow", [1, T], F32)
    rrow = b.sbuf("rrow", [1, T], F32)
    mrow = srow
    scrow = qrow
    psrow = b.psum("psrow", [1, 1024])
    psb1 = b.psum("psb1", [128, 512])
    psb2 = b.psum("psb2", [128, 512])
    banks = [b.psum(f"pb{i}", [128, 512]) for i in range(4)]

    ld = b.sem("ld")
    for c in range(12):
        b.load(tt[:, c * T:(c + 1) * T], ti[c * 128:(c + 1) * 128, :], sem=ld)
    for c in range(8):
        b.load(bx[:, c * T:(c + 1) * T], bxi[c * 128:(c + 1) * 128, :], sem=ld)
    load_weight_bf16(b, wt, wd, 12, D)
    nc.vector.wait_ge(ld.h, ld.n)
    nc.vector.memset(ones_col[:], 1.0)
    nc.vector.memset(ones_row[:], 1.0)
    nc.vector.memset(onesrb[:], 1.0)
    b.prep_consts([1e-5])
    b.barrier()

    def evac(oc, c0, w, ps, mm, ev):
        i = ev.n % 2
        nc.scalar.wait_ge(mm.h, mm.n)
        nc.scalar.wait_ge(b.sem("add").h, b.sem("add").n)
        b.sem("cp").inc(nc.scalar.activation(out=evt[i][:, 0:w], in_=ps[:, 0:w],
                                             func=AF.Identity))
        nc.vector.wait_ge(b.sem("cp").h, b.sem("cp").n)
        ev.inc(nc.vector.tensor_add(
            out=a1[:, oc * T + c0:oc * T + c0 + w],
            in0=bx[:, oc * T + c0:oc * T + c0 + w], in1=evt[i][:, 0:w]))
        b.sem("add").inc(nc.vector.engine_nop())

    emit_proj(b, wt, D, 12, 8,
              lambda ic, c0, w: tt[:, ic * T + c0:ic * T + c0 + w],
              evac, col_tiles(T), banks)
    b.barrier()
    sqb = b.sbuf("sqb", [128, 1024], BF16)
    ones_b = b.sbuf("ones_b", [128, 1], BF16)
    nc.vector.memset(ones_b[:], 1.0)
    b.barrier()
    emit_stats(b, lambda c: a1[:, c * T:(c + 1) * T], 8, T, ones_col, sq,
               psrow, srow, qrow, tag="f", bf_scratch=sqb, ones_bf=ones_b)
    b.barrier()
    emit_rstd(b, srow, qrow, T, 1.0 / D, 1e-5, rrow, mrow, scrow, tag="f")
    b.barrier()
    ap = b.sem("ap")
    st = b.sem("st")
    ld2 = b.sem("ld2")
    ap_marks = []
    for (c0, w) in col_tiles(T):
        emit_bcast(b, psb1, rrow[0:1, c0:c0 + w], w, ones_row, tag="f")
        bc = emit_bcast(b, psb2, mrow[0:1, c0:c0 + w], w, ones_row, tag="f")
        nc.vector.wait_ge(bc.h, bc.n)
        for c in range(8):
            i = c % 2
            if len(ap_marks) >= 2:
                nc.sync.wait_ge(ap.h, ap_marks[-2])
            b.load(x0t[i][:, 0:w], x0i[c * 128:(c + 1) * 128, c0:c0 + w],
                   sem=ld2, eng=nc.sync)
            nc.vector.wait_ge(ld2.h, ld2.n)
            if st.n >= 32:
                nc.vector.wait_ge(st.h, st.n - 16)
            g = c * T + c0
            ap.inc(nc.vector.tensor_sub(out=scr[:, 0:w], in0=a1[:, g:g + w],
                                        in1=psb2[:, 0:w]))
            ap.inc(nc.vector.tensor_mul(out=x1t[i][:, 0:w], in0=scr[:, 0:w],
                                        in1=psb1[:, 0:w]))
            ap.inc(nc.vector.tensor_add(out=x1t[i][:, 0:w], in0=x1t[i][:, 0:w],
                                        in1=x0t[i][:, 0:w]))
            ap_marks.append(ap.n)
            nc.gpsimd.wait_ge(ap.h, ap.n)
            b.store(x1o[c * 128:(c + 1) * 128, c0:c0 + w], x1t[i][:, 0:w], sem=st)
            tlo = A.get("x1tail")
            if tlo is not None and c0 == 1024:
                b.store(tlo[c * 128:(c + 1) * 128, 0:130],
                        x1t[i][:, 382:512], sem=st)
            elif tlo is not None and c0 == 1536:
                b.store(tlo[c * 128:(c + 1) * 128, 130:642],
                        x1t[i][:, 0:512], sem=st)
        nc.tensor.wait_ge(ap.h, ap.n)
    nc.gpsimd.wait_ge(st.h, st.n)
    return


def _run(bld, in_maps):
    res = run_bass_kernel_spmd(bld.nc, in_maps, core_ids=list(range(NCORES)))
    bld.close()
    return res.results


def _shard_tokens(x):
    out = []
    for c in range(NCORES):
        bb, t0 = c // 4, (c % 4) * T
        out.append(f32(x[bb, t0:t0 + T, :].T))
    return out


def _shard_tokens_halo(x, halo):
    out = []
    for c in range(NCORES):
        bb, t0 = c // 4, (c % 4) * T
        sl = x[bb, max(0, t0 - halo):t0 + T, :]
        if t0 - halo < 0:
            pad = np.zeros((halo - t0, x.shape[2]), sl.dtype)
            sl = np.concatenate([pad, sl], axis=0)
        out.append(f32(sl.T))
    return out


def _col_pack(v, n_chunks):
    return f32(np.asarray(v).reshape(n_chunks, 128).T)


def _cmask():
    out = []
    for c in range(NCORES):
        use = np.zeros(8, np.float32)
        lo = 0 if c < 4 else 4
        use[lo:c] = 1.0
        m = np.concatenate([use, 1.0 - use])
        out.append(f32(np.tile(m[None, :], (128, 1))))
    return out


def block1(d, x0):
    x0g = _shard_tokens_halo(x0, 4)
    n1w = _col_pack(d["L0_attn_norm1_w"], 8)
    r1 = _run(build_L01(), [{"x0g": x0g[c], "n1w": n1w} for c in range(NCORES)])
    pre1 = [r["pre1"] for r in r1]
    n1 = [r["n1"] for r in r1]

    wgt = nbf(np.asarray(d["L0_attn_W_gate"]).T)
    r2 = _run(build_L02(), [{"n1": np.ascontiguousarray(n1[c][:, 4:]),
                             "wgt": wgt} for c in range(NCORES)])
    gg = [r["gg"] for r in r2]

    wit = nbf(np.asarray(d["L0_attn_W_in"]).T)
    convw = f32(np.asarray(d["L0_attn_conv_w"])[:, 0, :].reshape(12, 128, 4)
                .transpose(1, 0, 2).reshape(128, 48))
    convb = _col_pack(d["L0_attn_conv_b"], 12)
    r3 = _run(build_L03(), [{"n1": n1[c], "wit": wit, "convw": convw,
                             "convb": convb} for c in range(NCORES)])
    ucb = [r["ucb"] for r in r3]

    rgin_t = nbf(np.asarray(d["L0_attn_rg_in_w"]).T)
    rggate_t = nbf(np.asarray(d["L0_attn_rg_gate_w"]).T)
    rginb = _col_pack(d["L0_attn_rg_in_b"], 12)
    rggateb = _col_pack(d["L0_attn_rg_gate_b"], 12)
    r45 = _run(build_L45(), [{"ucb": ucb[c], "rgin_t": rgin_t,
                              "rggate_t": rggate_t, "rginb": rginb,
                              "rggateb": rggateb} for c in range(NCORES)])
    sig = [r["sig"] for r in r45]
    srg = [r["srg"] for r in r45]

    lam = _col_pack(d["L0_attn_forget_lambda"], 12)
    r6 = _run(build_L06(), [{"ucb": ucb[c], "sig": sig[c], "srg": srg[c],
                             "lam": lam} for c in range(NCORES)])
    hl = [r["hl"] for r in r6]
    pp = [r["pp"] for r in r6]
    packs = f32(np.concatenate([r["pack"] for r in r6], axis=0))
    cmasks = _cmask()

    wot = nbf(np.asarray(d["L0_attn_W_out"]).T)
    n2w = _col_pack(d["L0_attn_norm2_w"], 8)
    r7 = _run(build_L07(), [{"hl": hl[c], "pp": pp[c], "gg": gg[c],
                             "pre1": np.ascontiguousarray(pre1[c][:, 4:]),
                             "packs": packs, "cmask": cmasks[c],
                             "wot": wot, "n2w": n2w} for c in range(NCORES)])
    bx = [r["bx"] for r in r7]
    n2 = [r["n2"] for r in r7]

    mgt = nbf(np.asarray(d["L0_attn_mlp_gate_w"]).T)
    mut = nbf(np.asarray(d["L0_attn_mlp_up_w"]).T)
    r8 = _run(build_L08(), [{"n2": n2[c], "mgt": mgt, "mut": mut}
                            for c in range(NCORES)])
    tt = [r["tt"] for r in r8]

    mdt = nbf(np.asarray(d["L0_attn_mlp_down_w"]).T)
    x0s = _shard_tokens(x0)
    r9 = _run(build_L09(), [{"tt": tt[c], "bx": bx[c], "x0": x0s[c],
                             "mdt": mdt} for c in range(NCORES)])
    return [r["x1"] for r in r9]


def _assemble(shards):
    out = np.empty((2, 8192, D), np.float32)
    for c in range(NCORES):
        bb, t0 = c // 4, (c % 4) * T
        out[bb, t0:t0 + T, :] = np.asarray(shards[c], np.float32).T
    return out


def kernel(**inputs):
    x0 = np.asarray(inputs["L0_attn_x"], np.float32)
    x1s = block1(inputs, x0)
    return _assemble(x1s)


HB = 642
WY = HB + T
KV = 2688
QEXT = 2176


def emit_L09b(b, A):
    W = WY
    nc = b.nc
    xg_i = A["x1g"]
    yo = A["y0"]

    xg = b.sbuf("xg", [128, 8 * W], F32)
    y = b.sbuf("ys", [128, 8 * W], BF16)
    ones_col = b.sbuf("ones_col", [128, 1], F32)
    ones_row = b.sbuf("ones_row", [1, 128], F32)
    onesrb = b.sbuf("onesrb", [1, 128], BF16)
    sq = b.sbuf("sq", [128, 1024], F32)
    scr = b.sbuf("scr", [128, 512], F32)
    srow = b.sbuf("srow", [1, W], F32)
    qrow = b.sbuf("qrow", [1, W], F32)
    rrow = b.sbuf("rrow", [1, W], F32)
    mrow = srow
    scrow = qrow
    psrow = b.psum("psrow", [1, 1024])
    psb1 = b.psum("psb1", [128, 512])
    psb2 = b.psum("psb2", [128, 512])

    ld = b.sem("ld")
    for c in range(8):
        b.load(xg[:, c * W:(c + 1) * W], xg_i[c * 128:(c + 1) * 128, :], sem=ld)
    nc.vector.wait_ge(ld.h, ld.n)
    nc.vector.memset(ones_col[:], 1.0)
    nc.vector.memset(ones_row[:], 1.0)
    nc.vector.memset(onesrb[:], 1.0)
    b.prep_consts([1e-5])
    b.barrier()
    sqb = b.sbuf("sqb", [128, 1024], BF16)
    ones_b = b.sbuf("ones_b", [128, 1], BF16)
    nc.vector.memset(ones_b[:], 1.0)
    b.barrier()
    emit_stats(b, lambda c: xg[:, c * W:(c + 1) * W], 8, W, ones_col, sq,
               psrow, srow, qrow, tag="a", tiles=col_tiles_ov(W),
               bf_scratch=sqb, ones_bf=ones_b)
    b.barrier()
    rbw_1148 = b.sbuf("rbw1148", [1, W], BF16)
    mbw_1148 = b.sbuf("mbw1148", [1, W], BF16)
    emit_rstd(b, srow, qrow, W, 1.0 / D, 1e-5, rrow, mrow, scrow, tag="a", rstd_bf=rbw_1148, mu_bf=mbw_1148)
    b.barrier()
    ap = b.sem("ap")
    for (c0, w) in col_tiles_ov(W):
        emit_bcast(b, psb1, rbw_1148[0:1, c0:c0 + w], w, onesrb, tag="a")
        bc = emit_bcast(b, psb2, mbw_1148[0:1, c0:c0 + w], w, onesrb, tag="a")
        nc.vector.wait_ge(bc.h, bc.n)
        for c in range(8):
            g = c * W + c0
            ap.inc(nc.vector.tensor_sub(out=scr[:, 0:w], in0=xg[:, g:g + w],
                                        in1=psb2[:, 0:w]))
            ap.inc(nc.vector.tensor_mul(out=y[:, g:g + w], in0=scr[:, 0:w],
                                        in1=psb1[:, 0:w]))
        nc.tensor.wait_ge(ap.h, ap.n)
    b.barrier()
    st = b.sem("st")
    for c in range(8):
        b.store(yo[c * 128:(c + 1) * 128, :], y[:, c * W:(c + 1) * W], sem=st)
    nc.gpsimd.wait_ge(st.h, st.n)
    return


def emit_L10(b, A):
    nc = b.nc
    yi = A["y0"]
    wq = A["wqkv"]
    qsc = A["qs"]
    ksc = A["ks"]
    tAq = A["tAq"]
    tBq = A["tBq"]
    tAk = A["tAk"]
    tBk = A["tBk"]
    qto = A["QT"]
    kto = A["KT"]
    vto = A["VT"]
    nbo = A["nb"]

    y = b.sbuf("ys", [128, 8 * WY], BF16)
    wt = b.sbuf("wt", [128, 8 * 1536], BF16)
    tq = [b.sbuf(f"tq{i}", [128, KV], BF16) for i in range(2)]
    tk = [b.sbuf(f"tk{i}", [128, KV], BF16) for i in range(2)]
    qv = b.sbuf("qv", [128, 1], F32)
    kv_ = b.sbuf("kv_", [128, 1], F32)
    F1s = [b.sbuf(f"F1{i}", [128, KV], F32) for i in range(2)]
    F2 = b.sbuf("F2", [128, KV], BF16)
    B1 = b.sbuf("B1", [128, KV], BF16)
    B2 = b.sbuf("B2", [128, KV], BF16)
    QT = b.sbuf("QTs", [128, 4 * KV], BF16)
    KT = b.sbuf("KTs", [128, 4 * KV], BF16)
    vt = [b.sbuf(f"vtile{i}", [128, 520], BF16) for i in range(2)]
    onesf = b.sbuf("onesf", [128, 1], F32)
    onesb = b.sbuf("onesb", [128, 1], BF16)
    selA = b.sbuf("selA", [1, 128], F32)
    selB = b.sbuf("selB", [1, 128], F32)
    nrmA = b.sbuf("nrmA", [1, KV], F32)
    nrmB = b.sbuf("nrmB", [1, KV], F32)
    nb = b.sbuf("nbs", [1, 64], F32)
    psA = b.psum("psA", [1, 512])
    psB = b.psum("psB", [1, 512])
    psb = b.psum("psb", [128, 512])
    banks = [b.psum(f"pb{i}", [128, 512]) for i in range(4)]

    ld = b.sem("ld")
    for c in range(8):
        b.load(y[:, c * WY:(c + 1) * WY], yi[c * 128:(c + 1) * 128, :], sem=ld)
    load_weight_bf16(b, wt, wq, 8, 1536)
    b.load(tq[0][:], tAq, sem=ld)
    b.load(tq[1][:], tBq, sem=ld)
    b.load(tk[0][:], tAk, sem=ld)
    b.load(tk[1][:], tBk, sem=ld)
    b.load(qv[:], qsc, sem=ld)
    b.load(kv_[:], ksc, sem=ld)
    nc.vector.wait_ge(ld.h, ld.n)
    nc.vector.memset(onesf[:], 1.0)
    nc.vector.memset(onesb[:], 1.0)
    nc.vector.memset(selA[0:1, 0:64], 1.0)
    nc.vector.memset(selA[0:1, 64:128], 0.0)
    nc.vector.memset(selB[0:1, 0:64], 0.0)
    nc.vector.memset(selB[0:1, 64:128], 1.0)
    nc.vector.memset(nb[:], 0.0)
    b.prep_consts([1e-24, -0.5])
    b.barrier()

    def rhs(ic, c0, w):
        off = 2 if ic < 4 else 1
        return y[:, ic * WY + off + c0: ic * WY + off + c0 + w]

    mm = b.sem("mm")
    ev = b.sem("ev")
    pr = b.sem("pr")
    st = b.sem("st")
    jobs = [(qk, hc) for qk in range(2) for hc in range(4)]
    evc = {}
    ev0 = ev.n
    gk_c = [0]

    def proj_stage(j):
        qk, hc = jobs[j]
        oc = qk * 4 + hc
        F = F1s[j % 2]
        prev_end = 0
        for (c0, w) in col_tiles_ov(KV):
            gk = gk_c[0]
            ps = banks[gk % 4]
            if gk >= 4:
                nc.tensor.wait_ge(ev.h, ev0 + gk - 3)
            ins = None
            for ic in range(8):
                ins = nc.tensor.matmul(
                    ps[:, 0:w],
                    wt[:, ic * 1536 + oc * 128: ic * 1536 + oc * 128 + 128],
                    rhs(ic, c0, w), start=(ic == 0), stop=(ic == 7))
            mm.inc(ins)
            nc.vector.wait_ge(mm.h, mm.n)
            e0 = max(c0, prev_end)
            ev.inc(nc.vector.tensor_copy(out=F[:, e0:c0 + w],
                                         in_=ps[:, e0 - c0:w]))
            prev_end = c0 + w
            gk_c[0] += 1
        evc[j] = ev.n

    def chain_stage(j):
        qk, hc = jobs[j]
        F1 = F1s[j % 2]
        dst = QT if qk == 0 else KT
        dsto = qto if qk == 0 else kto
        sc_col = qv if qk == 0 else kv_
        tA = tq[0] if qk == 0 else tk[0]
        tB = tq[1] if qk == 0 else tk[1]
        nc.scalar.wait_ge(ev.h, evc[j])
        nc.scalar.wait_ge(pr.h, pr.n)
        pr.inc(nc.scalar.activation(out=F2[:], in_=F1[:], func=AF.Square))
        nc.tensor.wait_ge(pr.h, pr.n)
        for ti2, (c0, w) in enumerate(col_tiles_ov(KV)):
            pr.inc(nc.tensor.matmul(psA[0:1, 0:w], onesb[0:64, :],
                                    F2[0:64, c0:c0 + w], start=True, stop=True))
            pr.inc(nc.tensor.matmul(psB[0:1, 0:w], onesb[64:128, :],
                                    F2[64:128, c0:c0 + w], start=True, stop=True))
            nc.vector.wait_ge(pr.h, pr.n)
            pr.inc(nc.vector.tensor_copy(out=nrmA[0:1, c0:c0 + w],
                                         in_=psA[0:1, 0:w]))
            pr.inc(nc.vector.tensor_copy(out=nrmB[0:1, c0:c0 + w],
                                         in_=psB[0:1, 0:w]))
            nc.tensor.wait_ge(pr.h, pr.n)
        for nr in (nrmA, nrmB):
            pr.inc(nc.vector.tensor_scalar_max(out=nr[:], in0=nr[:],
                                               scalar1=1e-24))
            nc.scalar.wait_ge(pr.h, pr.n)
            pr.inc(nc.scalar.activation(out=nr[:], in_=nr[:], func=AF.Ln))
            nc.scalar.wait_ge(pr.h, pr.n)
            pr.inc(nc.scalar.activation(out=nr[:], in_=nr[:], func=AF.Exp,
                                        scale=-0.5))
        nc.tensor.wait_ge(pr.h, pr.n)
        for (c0, w) in col_tiles_ov(KV):
            pr.inc(nc.tensor.matmul(psb[:, 0:w], selA[:], nrmA[0:1, c0:c0 + w],
                                    start=True, stop=False))
            pr.inc(nc.tensor.matmul(psb[:, 0:w], selB[:], nrmB[0:1, c0:c0 + w],
                                    start=False, stop=True))
            nc.vector.wait_ge(pr.h, pr.n)
            pr.inc(nc.vector.scalar_tensor_tensor(
                out=B1[:, c0:c0 + w], in0=F1[:, c0:c0 + w],
                scalar=sc_col[:], in1=psb[:, 0:w],
                op0=ALU.mult, op1=ALU.mult))
            nc.tensor.wait_ge(pr.h, pr.n)
        nc.scalar.wait_ge(pr.h, pr.n)
        pr.inc(nc.scalar.activation(out=F2[:], in_=B1[:], func=AF.Square))
        nc.tensor.wait_ge(pr.h, pr.n)
        for (c0, w) in col_tiles_ov(KV):
            pr.inc(nc.tensor.matmul(psA[0:1, 0:w], onesb[0:64, :],
                                    F2[0:64, c0:c0 + w], start=True, stop=True))
            pr.inc(nc.tensor.matmul(psB[0:1, 0:w], onesb[64:128, :],
                                    F2[64:128, c0:c0 + w], start=True, stop=True))
            nc.vector.wait_ge(pr.h, pr.n)
            pr.inc(nc.vector.tensor_copy(out=nrmA[0:1, c0:c0 + w],
                                         in_=psA[0:1, 0:w]))
            pr.inc(nc.vector.tensor_copy(out=nrmB[0:1, c0:c0 + w],
                                         in_=psB[0:1, 0:w]))
            nc.tensor.wait_ge(pr.h, pr.n)
        for g2, nr in ((0, nrmA), (1, nrmB)):
            col = qk * 32 + hc * 2 + g2
            pr.inc(nc.vector.tensor_reduce(
                out=nb[0:1, col:col + 1], in_=nr[0:1, :], axis=AX, op=ALU.max))
            nc.vector.wait_ge(pr.h, pr.n)
        pr.inc(nc.vector.tensor_copy(out=B2[0:32, :], in_=B1[32:64, :]))
        pr.inc(nc.vector.tensor_copy(out=B2[32:64, :], in_=B1[0:32, :]))
        pr.inc(nc.vector.tensor_copy(out=B2[64:96, :], in_=B1[96:128, :]))
        pr.inc(nc.vector.tensor_copy(out=B2[96:128, :], in_=B1[64:96, :]))
        F3 = F1
        nc.vector.wait_ge(pr.h, pr.n)
        pr.inc(nc.vector.tensor_mul(out=F2[:], in0=B2[:], in1=tB[:]))
        pr.inc(nc.vector.tensor_mul(out=F3[:], in0=B1[:], in1=tA[:]))
        nc.vector.wait_ge(pr.h, pr.n)
        pr.inc(nc.vector.tensor_add(out=dst[:, hc * KV:(hc + 1) * KV],
                                    in0=F3[:], in1=F2[:]))
        nc.gpsimd.wait_ge(pr.h, pr.n)
        b.store(dsto[hc * 128:(hc + 1) * 128, :],
                dst[:, hc * KV:(hc + 1) * KV], sem=st)

    if L10_PIPE == 0:
        for j in range(8):
            proj_stage(j)
            chain_stage(j)
    elif L10_PIPE == 1:
        proj_stage(0)
        proj_stage(1)
        chain_stage(0)
        chain_stage(1)
        for j in range(2, 8):
            proj_stage(j)
            chain_stage(j)
    elif L10_PIPE == 4:
        proj_stage(0)
        proj_stage(1)
        b.barrier()
        chain_stage(0)
        b.barrier()
        chain_stage(1)
        for j in range(2, 8):
            proj_stage(j)
            chain_stage(j)
    elif L10_PIPE == 6:
        proj_stage(0)
        proj_stage(1)
        b.barrier()
        chain_stage(0)
        chain_stage(1)
        for j in range(2, 8):
            proj_stage(j)
            chain_stage(j)
    elif L10_PIPE == 7:
        proj_stage(0)
        proj_stage(1)
        chain_stage(0)
        b.barrier()
        chain_stage(1)
        for j in range(2, 8):
            proj_stage(j)
            chain_stage(j)
    else:
        for j in range(8):
            proj_stage(j)
            if j >= 1:
                chain_stage(j - 1)
        chain_stage(7)
    mmv = b.sem("mmv")
    evv = b.sem("evv")
    stv = b.sem("stv")
    for tt in range(21):
        i = tt % 2
        if stv.n >= 32:
            nc.vector.wait_ge(stv.h, stv.n - 16)
        evv.inc(nc.vector.memset(vt[i][:], 1.0))
        ps = banks[tt % 4]
        if tt >= 4:
            nc.tensor.wait_ge(evv.h, evv.n - 2)
        ins = None
        for ic in range(8):
            off = 2 if ic < 4 else 1
            ins = nc.tensor.matmul(
                ps[:, 0:512],
                y[:, ic * WY + off + tt * 128: ic * WY + off + tt * 128 + 128],
                wt[:, ic * 1536 + 1024: ic * 1536 + 1536],
                start=(ic == 0), stop=(ic == 7))
        mmv.inc(ins)
        nc.vector.wait_ge(mmv.h, mmv.n)
        evv.inc(nc.vector.tensor_copy(
            out=vt[i][:].rearrange("p (h d) -> p h d", h=8)[:, :, 0:64],
            in_=ps[:, 0:512].rearrange("p (h d) -> p h d", h=8)))
        nc.gpsimd.wait_ge(evv.h, evv.n)
        b.store(vto[tt * 128:(tt + 1) * 128, :], vt[i][:], sem=stv)
    nc.gpsimd.wait_ge(pr.h, pr.n)
    b.store(nbo, nb[:], sem=stv)
    nc.gpsimd.wait_ge(stv.h, stv.n)
    nc.gpsimd.wait_ge(b.sem("st").h, b.sem("st").n)
    return


def emit_L11(b, A):
    nc = b.nc
    qti = A["QT"]
    kti = A["KT"]
    vti = A["VT"]
    yi = A["y0"]
    kc01 = A["keep01"]
    gmk = A["gmask"]
    nbi = A["nb"]
    wo = A["wo"]
    ymo = A["ymid"]

    QT = b.sbuf("QTs", [128, 4 * KV], BF16)
    KTe = b.sbuf("KTe", [128, 4 * KV], BF16)
    KTo = b.sbuf("KTo", [128, 4 * KV], BF16)
    vt = b.sbuf("vt", [128, 21 * 520], BF16)
    yts = b.sbuf("yts", [128, 2 * 8 * 512], BF16)
    kp = b.sbuf("kp", [128, 21], F32)
    gm = b.sbuf("gm", [128, 1], F32)
    wt = b.sbuf("wt", [128, 4 * D], BF16)
    ML = b.sbuf("ML", [128, 128], BF16)
    MU = b.sbuf("MU", [128, 128], BF16)
    nbrow = b.sbuf("nbrow", [1, 64], F32)
    bcol = b.sbuf("bcol", [128, 64], F32)
    eb = b.sbuf("ebb", [128, 8 * 512], BF16)
    M8 = b.sbuf("M8", [128, 8 * 512], BF16)
    zr = b.sbuf("zr", [1, 512], F32)
    zrb = b.sbuf("zrb", [1, 512], BF16)
    ones_rb = b.sbuf("ones_rb", [1, 64], BF16)
    rbs = [b.sbuf(f"rbs{i}", [64, 512], F32) for i in range(2)]
    obf = b.sbuf("obf", [128, 4 * QEXT], BF16)
    ymid = b.sbuf("ymids", [128, 8 * QEXT], BF16)
    gsc = b.sbuf("gsc", [128, 128], BF16)
    ones_row = b.sbuf("ones_row", [1, 128], F32)
    onesrb = b.sbuf("onesrb", [1, 128], BF16)
    pss = [b.psum(f"pss{i}", [128, 1024]) for i in range(2)]
    pso = [b.psum(f"pso{i}", [65, 512]) for i in range(2)]
    psz = [b.psum(f"psz{i}", [64, 512]) for i in range(2)]
    banks = [pss[0][:, 0:512], pss[1][:, 0:512]]

    mlref = np.tril(np.ones((128, 128), np.float32))
    muref = np.triu(np.ones((128, 128), np.float32))
    mlc = b.nc.inline_tensor(np.ascontiguousarray(mlref.astype(NBF)), b.pfx + "mlc")
    muc = b.nc.inline_tensor(np.ascontiguousarray(muref.astype(NBF)), b.pfx + "muc")

    ld = b.sem("ld")
    kzz = b.sem("kzz")
    kzz.inc(nc.vector.memset(KTe[64:128, :], 0.0))
    kzz.inc(nc.vector.memset(KTo[0:64, :], 0.0))
    nc.sync.wait_ge(kzz.h, kzz.n)
    for c in range(4):
        b.load(QT[:, c * KV:(c + 1) * KV], qti[c * 128:(c + 1) * 128, :], sem=ld)
        b.load(KTe[0:64, c * KV:(c + 1) * KV],
               kti[c * 128:c * 128 + 64, :], sem=ld)
        b.load(KTo[64:128, c * KV:(c + 1) * KV],
               kti[c * 128 + 64:(c + 1) * 128, :], sem=ld)
    for tt in range(21):
        b.load(vt[:, tt * 520:(tt + 1) * 520], vti[tt * 128:(tt + 1) * 128, :],
               sem=ld)
    b.load(kp[:], kc01, sem=ld)
    b.load(gm[:], gmk, sem=ld)
    b.load(ML[:], mlc.ap(), sem=ld)
    b.load(MU[:], muc.ap(), sem=ld)
    b.load(nbrow[:], nbi, sem=ld)
    load_weight_bf16(b, wt, wo, 4, D)
    nc.vector.wait_ge(ld.h, ld.n)
    nc.vector.memset(ones_row[:], 1.0)
    nc.vector.memset(onesrb[:], 1.0)
    nc.vector.memset(ones_rb[:], 1.0)
    nc.vector.memset(M8[:], 0.0)
    mz = b.sem("mz")
    mz.inc(nc.vector.engine_nop())
    nc.vector.wait_ge(mz.h, mz.n)
    for jc in range(8):
        for j4 in range(4):
            r = jc - j4
            dst = M8[:, jc * 512 + j4 * 128: jc * 512 + j4 * 128 + 128]
            if 1 <= r <= 3:
                nc.vector.memset(dst, 1.0)
            elif r == 0:
                nc.vector.tensor_copy(out=dst, in_=ML[:])
            elif r == 4:
                nc.vector.tensor_copy(out=dst, in_=MU[:])
    b.prep_consts([1e-30, -1.0])

    pr = b.sem("pr")
    nc.vector.wait_ge(b.sem("ld").h, b.sem("ld").n)
    pr.inc(nc.vector.tensor_mul(out=nbrow[0:1, 48:56], in0=nbrow[0:1, 0:8],
                                in1=nbrow[0:1, 32:40]))
    nc.scalar.wait_ge(pr.h, pr.n)
    pr.inc(nc.scalar.activation(out=nbrow[0:1, 48:56], in_=nbrow[0:1, 48:56],
                                func=AF.Sqrt))
    nc.vector.wait_ge(pr.h, pr.n)
    pr.inc(nc.vector.tensor_scalar_mul(out=nbrow[0:1, 48:56],
                                       in0=nbrow[0:1, 48:56], scalar1=-8.0))
    nc.vector.wait_ge(pr.h, pr.n)
    nc.tensor.wait_ge(pr.h, pr.n)
    pr.inc(nc.tensor.matmul(psz[0][0:64, 0:64], ones_row[0:1, 0:64],
                            nbrow[0:1, 0:64], start=True, stop=True))
    nc.vector.wait_ge(pr.h, pr.n)
    pr.inc(nc.vector.tensor_copy(out=bcol[0:64, :], in_=psz[0][0:64, 0:64]))
    pr.inc(nc.vector.tensor_copy(out=bcol[64:128, :], in_=psz[0][0:64, 0:64]))
    nc.vector.wait_ge(pr.h, pr.n)
    b.barrier()

    mm = b.sem("mm")
    ex = b.sem("ex")
    ms = b.sem("ms")
    av = b.sem("av")
    no = b.sem("no")
    ex_marks = []
    no_marks = {}
    pp = 0

    def norm_group(stp, h, g, w):
        hc, r0 = h // 2, 64 * (h % 2)
        ob = pso[stp % 2]
        pz = psz[stp % 2]
        nc.vector.wait_ge(av.h, av.n)
        no.inc(nc.vector.tensor_copy(out=zr[0:1, 0:w], in_=ob[64:65, 0:w]))
        nc.vector.wait_ge(no.h, no.n)
        no.inc(nc.vector.tensor_scalar_max(out=zr[0:1, 0:w], in0=zr[0:1, 0:w],
                                           scalar1=1e-30))
        nc.scalar.wait_ge(no.h, no.n)
        no.inc(nc.scalar.activation(out=zr[0:1, 0:w], in_=zr[0:1, 0:w],
                                    func=AF.Ln))
        nc.scalar.wait_ge(no.h, no.n)
        no.inc(nc.scalar.activation(out=zrb[0:1, 0:w], in_=zr[0:1, 0:w],
                                    func=AF.Exp, scale=-1.0))
        nc.tensor.wait_ge(no.h, no.n)
        no.inc(nc.tensor.matmul(pz[0:64, 0:w], ones_rb[0:1, 0:64],
                                zrb[0:1, 0:w], start=True, stop=True))
        nc.vector.wait_ge(no.h, no.n)
        no.inc(nc.vector.tensor_copy(out=rbs[stp % 2][:, 0:w],
                                     in_=pz[0:64, 0:w]))
        nc.vector.wait_ge(no.h, no.n)
        no.inc(nc.vector.tensor_mul(
            out=obf[r0:r0 + 64, hc * QEXT + g * 512: hc * QEXT + g * 512 + w],
            in0=ob[0:64, 0:w], in1=rbs[stp % 2][:, 0:w]))
        no_marks[stp] = no.n

    stp = 0
    for h in range(8):
        hc, r0 = h // 2, 64 * (h % 2)
        KTz = KTe if h % 2 == 0 else KTo
        for g in range(5):
            nch = 8 if g < 4 else 5
            qw = 512 if g < 4 else 128
            q0 = hc * KV + 512 + g * 512
            ms_marks = []
            for cc in range((nch + 1) // 2):
                jcs = [2 * cc + j for j in range(2) if 2 * cc + j < nch]
                if pp >= 2:
                    nc.tensor.wait_ge(ex.h, ex_marks[pp - 2])
                for j, jc in enumerate(jcs):
                    kci = g * 4 + jc
                    mm.inc(nc.tensor.matmul(
                        pss[pp % 2][:, j * qw:(j + 1) * qw],
                        KTz[:, hc * KV + kci * 128: hc * KV + kci * 128 + 128],
                        QT[:, q0: q0 + qw],
                        start=True, stop=True))
                pp += 1
                nc.scalar.wait_ge(mm.h, mm.n)
                ex.inc(nc.scalar.activation(
                    out=eb[:, 2 * cc * qw: 2 * cc * qw + len(jcs) * qw],
                    in_=pss[pp % 2 ^ 1][:, 0:len(jcs) * qw], func=AF.Exp,
                    bias=bcol[:, 48 + h:49 + h], scale=8.0))
                ex_marks.append(ex.n)
                nc.vector.wait_ge(ex.h, ex.n)
                for jc in jcs:
                    kci = g * 4 + jc
                    mrow = jc
                    ms.inc(nc.vector.scalar_tensor_tensor(
                        out=eb[:, jc * qw:(jc + 1) * qw],
                        in0=M8[:, mrow * 512: mrow * 512 + qw],
                        scalar=kp[:, kci:kci + 1],
                        in1=eb[:, jc * qw:(jc + 1) * qw],
                        op0=ALU.mult, op1=ALU.mult))
                    ms_marks.append(ms.n)
            for jc in range(nch):
                kci = g * 4 + jc
                nc.tensor.wait_ge(ms.h, ms_marks[jc])
                if jc == 0 and stp >= 2:
                    nc.tensor.wait_ge(no.h, no_marks[stp - 2])
                ins = nc.tensor.matmul(
                    pso[stp % 2][:, 0:qw],
                    vt[:, kci * 520 + h * 65: kci * 520 + h * 65 + 65],
                    eb[:, jc * qw:(jc + 1) * qw],
                    start=(jc == 0), stop=(jc == nch - 1))
            av.inc(ins)
            norm_group(stp, h, g, qw)
            stp += 1
    nc.tensor.wait_ge(no.h, no.n)
    b.barrier()

    mm2 = b.sem("mm2")
    ev2 = b.sem("ev2")
    ld2 = b.sem("ld2")
    wo_tiles = col_tiles_ov(QEXT)

    def load_ytiles(ti):
        c0, w = wo_tiles[ti]
        for oc in range(8):
            b.load(yts[:, (ti % 2) * 4096 + oc * 512:
                        (ti % 2) * 4096 + oc * 512 + w],
                   yi[oc * 128:(oc + 1) * 128, 514 + c0: 514 + c0 + w],
                   sem=ld2)
        return ld2.n

    ymarks = [load_ytiles(0)]
    k = 0
    for ti, (c0, w) in enumerate(wo_tiles):
        if ti + 1 < len(wo_tiles):
            nc.sync.wait_ge(ev2.h, ev2.n)
            ymarks.append(load_ytiles(ti + 1))
        nc.vector.wait_ge(ld2.h, ymarks[ti])
        for oc in range(8):
            ps = banks[k % 2]
            if k >= 2:
                nc.tensor.wait_ge(ev2.h, ev2.n - 1)
            ins = None
            for ic in range(4):
                ins = nc.tensor.matmul(
                    ps[:, 0:w],
                    wt[:, ic * D + oc * 128: ic * D + oc * 128 + 128],
                    obf[:, ic * QEXT + c0: ic * QEXT + c0 + w],
                    start=(ic == 0), stop=(ic == 3))
            mm2.inc(ins)
            nc.vector.wait_ge(mm2.h, mm2.n)
            ev2.inc(nc.vector.tensor_add(
                out=ymid[:, oc * QEXT + c0: oc * QEXT + c0 + w],
                in0=yts[:, (ti % 2) * 4096 + oc * 512:
                        (ti % 2) * 4096 + oc * 512 + w],
                in1=ps[:, 0:w]))
            k += 1
    b.barrier()
    gz = b.sem("gz")
    for oc in range(8):
        gz.inc(nc.vector.tensor_scalar(
            out=gsc[:], in0=ymid[:, oc * QEXT: oc * QEXT + 128],
            scalar1=gm[:], scalar2=None, op0=ALU.mult))
        nc.vector.wait_ge(gz.h, gz.n)
        gz.inc(nc.vector.tensor_copy(out=ymid[:, oc * QEXT: oc * QEXT + 128],
                                     in_=gsc[:]))
        nc.vector.wait_ge(gz.h, gz.n)
    b.barrier()
    st = b.sem("st")
    for oc in range(8):
        b.store(ymo[oc * 128:(oc + 1) * 128, :],
                ymid[:, oc * QEXT:(oc + 1) * QEXT], sem=st)
    nc.gpsimd.wait_ge(st.h, st.n)
    return


HID2 = 2730
RAG = [128] * 21 + [42]


def emit_L12a(b, A):
    nc = b.nc
    ymi = A["ymid"]
    lgc = A["lg"]
    lbc = A["lb"]
    w1 = A["w1"]
    b1c = A["b1c"]
    to = A["tt"]

    ym = b.sbuf("ym", [128, 8 * QEXT], BF16)
    wt = b.sbuf("wt", [128, 8 * 2 * HID2], BF16)
    nb = b.sbuf("nb", [128, 8 * T], BF16)
    lgv = b.sbuf("lgv", [128, 8], F32)
    lbv = b.sbuf("lbv", [128, 8], F32)
    b1v = b.sbuf("b1v", [128, 44], F32)
    ones_bf = b.sbuf("ones_bf", [128, 1], BF16)
    ones_f = b.sbuf("ones_f", [128, 1], F32)
    ones_row = b.sbuf("ones_row", [1, 128], F32)
    onesrb = b.sbuf("onesrb", [1, 128], BF16)
    sq = b.sbuf("sq", [128, 1024], F32)
    scr = b.sbuf("scr", [128, 512], F32)
    srow = b.sbuf("srow", [1, T], F32)
    qrow = b.sbuf("qrow", [1, T], F32)
    rrow = b.sbuf("rrow", [1, T], F32)
    ab = [b.sbuf(f"ab{i}", [128, 512], BF16) for i in range(2)]
    gb = [b.sbuf(f"gb{i}", [128, 512], BF16) for i in range(2)]
    ot = [b.sbuf(f"ot{i}", [128, 512], BF16) for i in range(2)]
    psrow = b.psum("psrow", [1, 1024])
    psb1 = b.psum("psb1", [128, 512])
    psb2 = b.psum("psb2", [128, 512])
    banks = [b.psum(f"pb{i}", [128, 512]) for i in range(4)]

    ld = b.sem("ld")
    for c in range(8):
        b.load(ym[:, c * QEXT:(c + 1) * QEXT], ymi[c * 128:(c + 1) * 128, :],
               sem=ld)
    b.load(lgv[:], lgc, sem=ld)
    b.load(lbv[:], lbc, sem=ld)
    b.load(b1v[:], b1c, sem=ld)
    load_weight_bf16(b, wt, w1, 8, 2 * HID2)
    nc.vector.wait_ge(ld.h, ld.n)
    nc.vector.memset(ones_bf[:], 1.0)
    nc.vector.memset(ones_f[:], 1.0)
    nc.vector.memset(ones_row[:], 1.0)
    nc.vector.memset(onesrb[:], 1.0)
    b.prep_consts([1e-5])
    b.barrier()

    def h2(ic, c0, w):
        off = 128 if ic < 4 else 127
        return ym[:, ic * QEXT + off + c0: ic * QEXT + off + c0 + w]

    emit_stats(b, None, 8, T, ones_bf, sq, psrow, srow, qrow, tag="a",
               chunk_src=h2, ones_sq=ones_f)
    b.barrier()
    rbw_1751 = b.sbuf("rbw1751", [1, T], BF16)
    mbw_1751 = b.sbuf("mbw1751", [1, T], BF16)
    emit_rstd(b, srow, qrow, T, 1.0 / D, 1e-5, rrow, srow, qrow, tag="a", rstd_bf=rbw_1751, mu_bf=mbw_1751)
    b.barrier()
    ap = b.sem("ap")
    for (c0, w) in col_tiles(T):
        emit_bcast(b, psb1, rbw_1751[0:1, c0:c0 + w], w, onesrb, tag="a")
        bc = emit_bcast(b, psb2, mbw_1751[0:1, c0:c0 + w], w, onesrb, tag="a")
        nc.vector.wait_ge(bc.h, bc.n)
        for c in range(8):
            ap.inc(nc.vector.tensor_sub(out=scr[:, 0:w], in0=h2(c, c0, w),
                                        in1=psb2[:, 0:w]))
            ap.inc(nc.vector.tensor_mul(out=scr[:, 0:w], in0=scr[:, 0:w],
                                        in1=psb1[:, 0:w]))
            nc.vector.wait_ge(ap.h, ap.n)
            ap.inc(nc.vector.tensor_scalar(
                out=nb[:, c * T + c0:c * T + c0 + w], in0=scr[:, 0:w],
                scalar1=lgv[:, c:c + 1], scalar2=lbv[:, c:c + 1],
                op0=ALU.mult, op1=ALU.add))
        nc.tensor.wait_ge(ap.h, ap.n)
    b.barrier()

    mm = b.sem("mm")
    eva = b.sem("eva")
    mu = b.sem("mu")
    st = b.sem("st")
    kk = 0
    for (c0, w) in col_tiles(T):
        for i, kp_ in enumerate(RAG):
            bi = kk // 2 % 2
            for half, buf in ((0, ab), (1, gb)):
                ps = banks[kk % 4]
                if kk >= 4:
                    nc.tensor.wait_ge(eva.h, eva.n - 3)
                base = half * HID2 + i * 128
                ins = None
                for ic in range(8):
                    ins = nc.tensor.matmul(
                        ps[0:kp_, 0:w],
                        wt[:, ic * 2 * HID2 + base: ic * 2 * HID2 + base + kp_],
                        nb[:, ic * T + c0: ic * T + c0 + w],
                        start=(ic == 0), stop=(ic == 7))
                mm.inc(ins)
                nc.scalar.wait_ge(mm.h, mm.n)
                if mu.n >= 2:
                    nc.scalar.wait_ge(mu.h, mu.n - 1)
                bcoln = half * 22 + i
                eva.inc(nc.scalar.activation(
                    out=buf[bi][0:kp_, 0:w], in_=ps[0:kp_, 0:w],
                    func=(AF.Gelu if half == 1 else AF.Identity),
                    bias=b1v[0:kp_, bcoln:bcoln + 1]))
                kk += 1
            nc.vector.wait_ge(eva.h, eva.n)
            if st.n >= 32:
                nc.vector.wait_ge(st.h, st.n - 16)
            mu.inc(nc.vector.tensor_mul(out=ot[bi][0:kp_, 0:w],
                                        in0=ab[bi][0:kp_, 0:w],
                                        in1=gb[bi][0:kp_, 0:w]))
            nc.gpsimd.wait_ge(mu.h, mu.n)
            b.store(to[i * 128:i * 128 + kp_, c0:c0 + w],
                    ot[bi][0:kp_, 0:w], sem=st)
    nc.gpsimd.wait_ge(st.h, st.n)
    return


def emit_L12b(b, A, final):
    nc = b.nc
    ti = A["tt"]
    ymi = A["ymid"]
    w2 = A["w2"]
    b2c = A["b2c"]
    if final:
        x1i = A["x1"]
        x2o = A["x2"]
    else:
        yo = A["yout"]

    wt = b.sbuf("wt", [128, 22 * D], BF16)
    ym = b.sbuf("ym", [128, 8 * QEXT], BF16)
    tb = [b.sbuf("tb0", [128, 22 * 512], BF16)]
    b2v = b.sbuf("b2v", [128, 8], F32)
    yv = b.sbuf("yv", [128, 8 * T], F32 if final else BF16)
    ones_col = b.sbuf("ones_col", [128, 1], F32)
    ones_row = b.sbuf("ones_row", [1, 128], F32)
    onesrb = b.sbuf("onesrb", [1, 128], BF16)
    sq = b.sbuf("sq", [128, 1024], F32)
    banks = [b.psum(f"pb{i}", [128, 512]) for i in range(4)]
    if final:
        srow = b.sbuf("srow", [1, T], F32)
        qrow = b.sbuf("qrow", [1, T], F32)
        rrow = b.sbuf("rrow", [1, T], F32)
        x1t = [b.sbuf(f"x1t{i}", [128, 512], F32) for i in range(2)]
        x2t = [b.sbuf(f"x2t{i}", [128, 512], F32) for i in range(2)]
        scr = sq[:, 512:1024]
        psrow = b.psum("psrow", [1, 1024])
        psb1 = b.psum("psb1", [128, 512])
        psb2 = b.psum("psb2", [128, 512])

    ld = b.sem("ld")
    ldt = b.sem("ldt")
    for c in range(8):
        b.load(ym[:, c * QEXT:(c + 1) * QEXT], ymi[c * 128:(c + 1) * 128, :],
               sem=ld)
    b.load(b2v[:], b2c, sem=ld)
    load_weight_bf16(b, wt, w2, 22, D, kparts=RAG)

    def load_tct(ct_i, c0):
        i = 0
        row = 0
        for j, kp_ in enumerate(RAG):
            b.load(tb[i][0:kp_, j * 512:j * 512 + 512],
                   ti[row:row + kp_, c0:c0 + 512], sem=ldt)
            row += kp_
        return ldt.n

    marks = [load_tct(0, 0)]
    nc.vector.wait_ge(ld.h, ld.n)
    nc.vector.memset(ones_col[:], 1.0)
    nc.vector.memset(ones_row[:], 1.0)
    nc.vector.memset(onesrb[:], 1.0)
    b.prep_consts([1e-5])
    b.barrier()

    mm = b.sem("mm")
    ev = b.sem("ev")
    tiles = col_tiles(T)
    for ct_i, (c0, w) in enumerate(tiles):
        i = 0
        if ct_i > 0:
            nc.sync.wait_ge(mm.h, mm.n)
            marks.append(load_tct(ct_i, c0))
        nc.tensor.wait_ge(ldt.h, marks[ct_i])
        for oc in range(8):
            ps = banks[oc % 4]
            if ct_i * 8 + oc >= 4:
                nc.tensor.wait_ge(ev.h, ev.n - 3)
            ins = None
            for j, kp_ in enumerate(RAG):
                ins = nc.tensor.matmul(
                    ps[:, 0:w],
                    wt[0:kp_, j * D + oc * 128: j * D + oc * 128 + 128],
                    tb[i][0:kp_, j * 512:j * 512 + w],
                    start=(j == 0), stop=(j == 21))
            mm.inc(ins)
            nc.vector.wait_ge(mm.h, mm.n)
            ev.inc(nc.vector.tensor_scalar(
                out=sq[:, 0:w], in0=ps[:, 0:w], scalar1=b2v[:, oc:oc + 1],
                scalar2=None, op0=ALU.add))
            nc.vector.wait_ge(ev.h, ev.n)
            ev.inc(nc.vector.tensor_add(
                out=yv[:, oc * T + c0: oc * T + c0 + w],
                in0=sq[:, 0:w],
                in1=ym[:, oc * QEXT + 128 + c0: oc * QEXT + 128 + c0 + w]))
    b.barrier()
    st = b.sem("st")
    if not final:
        tlo = A.get("ytail")
        for c in range(8):
            b.store(yo[c * 128:(c + 1) * 128, :], yv[:, c * T:(c + 1) * T], sem=st)
            if tlo is not None:
                b.store(tlo[c * 128:(c + 1) * 128, :],
                        yv[:, c * T + 1406:(c + 1) * T], sem=st)
        nc.gpsimd.wait_ge(st.h, st.n)
        return
    sqb = b.sbuf("sqb", [128, 1024], BF16)
    ones_b = b.sbuf("ones_b", [128, 1], BF16)
    nc.vector.memset(ones_b[:], 1.0)
    b.barrier()
    emit_stats(b, lambda c: yv[:, c * T:(c + 1) * T], 8, T, ones_col, sq,
               psrow, srow, qrow, tag="f", bf_scratch=sqb, ones_bf=ones_b)
    b.barrier()
    rbw_1919 = b.sbuf("rbw1919", [1, T], BF16)
    emit_rstd(b, srow, qrow, T, 1.0 / D, 1e-5, rrow, srow, qrow, tag="f", rstd_bf=rbw_1919)
    b.barrier()
    ap = b.sem("ap")
    ld2 = b.sem("ld2")
    ap_marks = []
    for (c0, w) in col_tiles(T):
        emit_bcast(b, psb1, rbw_1919[0:1, c0:c0 + w], w, onesrb, tag="f")
        bc = emit_bcast(b, psb2, srow[0:1, c0:c0 + w], w, ones_row, tag="f")
        nc.vector.wait_ge(bc.h, bc.n)
        for c in range(8):
            i = c % 2
            if len(ap_marks) >= 2:
                nc.sync.wait_ge(ap.h, ap_marks[-2])
            b.load(x1t[i][:, 0:w], x1i[c * 128:(c + 1) * 128, c0:c0 + w],
                   sem=ld2, eng=nc.sync)
            nc.vector.wait_ge(ld2.h, ld2.n)
            if st.n >= 32:
                nc.vector.wait_ge(st.h, st.n - 16)
            g = c * T + c0
            ap.inc(nc.vector.tensor_sub(out=scr[:, 0:w], in0=yv[:, g:g + w],
                                        in1=psb2[:, 0:w]))
            ap.inc(nc.vector.tensor_mul(out=x2t[i][:, 0:w], in0=scr[:, 0:w],
                                        in1=psb1[:, 0:w]))
            ap.inc(nc.vector.tensor_add(out=x2t[i][:, 0:w], in0=x2t[i][:, 0:w],
                                        in1=x1t[i][:, 0:w]))
            ap_marks.append(ap.n)
            nc.gpsimd.wait_ge(ap.h, ap.n)
            b.store(x2o[c * 128:(c + 1) * 128, c0:c0 + w], x2t[i][:, 0:w], sem=st)
        nc.tensor.wait_ge(ap.h, ap.n)
    nc.gpsimd.wait_ge(st.h, st.n)
    return


def _shard_tokens_halo_t(x, halo):
    out = []
    for c in range(NCORES):
        bb, t0 = c // 4, (c % 4) * T
        sl = x[bb, max(0, t0 - halo):t0 + T, :]
        if t0 - halo < 0:
            pad = np.zeros((halo - t0, x.shape[2]), sl.dtype)
            sl = np.concatenate([pad, sl], axis=0)
        out.append(np.ascontiguousarray(sl.T))
    return out


def _xpos_tables(core):
    t0 = (core % 4) * T
    t = np.arange(t0 - 640, t0 + T, dtype=np.float32)
    t = np.maximum(t, 0.0)
    half_idx = np.arange(0, 64, 2, dtype=np.float32)
    inv_freq = 1.0 / 10000.0 ** (half_idx / 64.0)
    base = (half_idx + 0.4 * 64) / (1.4 * 64)
    power = (t - 4096.0) / 256.0
    f = t[None, :] * inv_freq[:, None]
    sc = base[:, None] ** power[None, :]
    cos = np.cos(f)
    sin = np.sin(f)
    cos64 = np.concatenate([cos, cos], axis=0)
    sin64 = np.concatenate([sin, sin], axis=0)
    sc64 = np.concatenate([sc, sc], axis=0)
    sgn = np.concatenate([-np.ones((32, 1), np.float32),
                          np.ones((32, 1), np.float32)], axis=0)
    Aq = cos64 * sc64
    Bq = sgn * sin64 * sc64
    Ak = cos64 / sc64
    Bk = sgn * sin64 / sc64
    def dup(m):
        return nbf(np.concatenate([m, m], axis=0))
    return dup(Aq), dup(Bq), dup(Ak), dup(Bk)


def _rag_cols(v):
    out = np.zeros((128, 22), np.float32)
    r = 0
    for j, kp_ in enumerate(RAG):
        out[0:kp_, j] = np.asarray(v[r:r + kp_], np.float32)
        r += kp_
    return out


def block2(d, x1_full):
    y = None
    x1g = _shard_tokens_halo(x1_full, HB)
    r = _run(build_L09b(), [{"x1g": x1g[c]} for c in range(NCORES)])
    yg = [r_["y0"] for r_ in r]

    tabs = [_xpos_tables(c) for c in range(NCORES)]
    keep = []
    gmsk = []
    for c in range(NCORES):
        k01 = np.ones((128, 21), np.float32)
        if c % 4 == 0:
            k01[:, 0:5] = 0.0
        keep.append(f32(k01))
        gmsk.append(f32(np.full((128, 1), 0.0 if c % 4 == 0 else 1.0)))

    for l in range(2):
        wq = nbf(np.asarray(d["L0_ffn_wqkv"][l]))
        qs = f32(np.tile(np.asarray(d["L0_ffn_q_scale"][l]), 2)[:, None])
        ks = f32(np.tile(np.asarray(d["L0_ffn_k_scale"][l]), 2)[:, None])
        r10 = _run(build_L10(), [{"y0": yg[c], "wqkv": wq, "qs": qs, "ks": ks,
                                  "tAq": tabs[c][0], "tBq": tabs[c][1],
                                  "tAk": tabs[c][2], "tBk": tabs[c][3]}
                                 for c in range(NCORES)])
        wo = nbf(np.asarray(d["L0_ffn_wo"][l]))
        r11 = _run(build_L11(), [{"QT": r10[c]["QT"], "KT": r10[c]["KT"],
                                  "VT": r10[c]["VT"], "y0": yg[c],
                                  "keep01": keep[c], "gmask": gmsk[c],
                                  "nb": r10[c]["nb"], "wo": wo}
                                 for c in range(NCORES)])
        ym = [r_["ymid"] for r_ in r11]
        lg = _col_pack(d["L0_ffn_ln_g"][l], 8)
        lb = _col_pack(d["L0_ffn_ln_b"][l], 8)
        w1 = nbf(np.asarray(d["L0_ffn_w1"][l]))
        b1v = np.asarray(d["L0_ffn_b1"][l])
        b1c = np.concatenate([_rag_cols(b1v[0:HID2]), _rag_cols(b1v[HID2:])],
                             axis=1)
        r12a = _run(build_L12a(), [{"ymid": ym[c], "lg": lg, "lb": lb,
                                    "w1": w1, "b1c": f32(b1c)}
                                   for c in range(NCORES)])
        w2 = nbf(np.asarray(d["L0_ffn_w2"][l]))
        b2c = _col_pack(d["L0_ffn_b2"][l], 8)
        if l == 0:
            r12b = _run(build_L12b(False),
                        [{"tt": r12a[c]["tt"], "ymid": ym[c], "w2": w2,
                          "b2c": b2c} for c in range(NCORES)])
            y_full = np.empty((2, 8192, D), NBF)
            for c in range(NCORES):
                bb, t0 = c // 4, (c % 4) * T
                y_full[bb, t0:t0 + T, :] = np.asarray(r12b[c]["yout"]).T
            yg = _shard_tokens_halo_t(y_full, HB)
        else:
            x1s = _shard_tokens(x1_full)
            r12b = _run(build_L12b(True),
                        [{"tt": r12a[c]["tt"], "ymid": ym[c], "w2": w2,
                          "b2c": b2c, "x1": x1s[c]} for c in range(NCORES)])
            return [r_["x2"] for r_ in r12b]


def emit_L13(b, A):
    nc = b.nc
    xi = A["x2"]
    wq = A["wqkv3"]
    tA = A["tA3"]
    tB = A["tB3"]
    qto = A["QT3"]
    kto = A["KT3"]
    vto = A["VT3"]
    nbo = A["nb3"]

    xg = b.sbuf("xg", [128, 8 * T], F32)
    n3 = b.sbuf("n3", [128, 8 * T], BF16)
    wt = b.sbuf("wt", [128, 8 * 3 * D], BF16)
    tAb = b.sbuf("tAb", [128, T], BF16)
    tBb = b.sbuf("tBb", [128, T], BF16)
    F1 = b.sbuf("F1", [128, T], F32)
    B1 = b.sbuf("B1", [128, T], BF16)
    B2 = b.sbuf("B2", [128, T], BF16)
    QK = B2
    vt = [b.sbuf(f"vtile{i}", [128, 1040], BF16) for i in range(2)]
    nb = b.sbuf("nbs", [1, 64], F32)
    onesf = b.sbuf("onesf", [128, 1], F32)
    ones_row = b.sbuf("ones_row", [1, 128], F32)
    onesrb = b.sbuf("onesrb", [1, 128], BF16)
    sq = b.sbuf("sq", [128, 1024], F32)
    scr = sq[:, 512:1024]
    srow = b.sbuf("srow", [1, T], F32)
    qrow = b.sbuf("qrow", [1, T], F32)
    rrow = b.sbuf("rrow", [1, T], F32)
    nrmA = srow
    nrmB = qrow
    psA = b.psum("psA", [1, 512])
    psB = b.psum("psB", [1, 512])
    psrow = b.psum("psrow", [1, 512])
    psb1 = b.psum("psb1", [128, 512])
    psb2 = b.psum("psb2", [128, 512])
    banks = [b.psum(f"pb{i}", [128, 512]) for i in range(3)]

    ld = b.sem("ld")
    for c in range(8):
        b.load(xg[:, c * T:(c + 1) * T], xi[c * 128:(c + 1) * 128, :], sem=ld)
    b.load(tAb[:], tA, sem=ld)
    b.load(tBb[:], tB, sem=ld)
    load_weight_bf16(b, wt, wq, 8, 3 * D)
    nc.vector.wait_ge(ld.h, ld.n)
    nc.vector.memset(onesf[:], 1.0)
    nc.vector.memset(ones_row[:], 1.0)
    nc.vector.memset(onesrb[:], 1.0)
    nc.vector.memset(nb[:], 0.0)
    b.prep_consts([1e-5])
    b.barrier()

    sqb = b.sbuf("sqb", [128, 1024], BF16)
    ones_b = b.sbuf("ones_b", [128, 1], BF16)
    nc.vector.memset(ones_b[:], 1.0)
    b.barrier()
    emit_stats(b, lambda c: xg[:, c * T:(c + 1) * T], 8, T, onesf, sq,
               psrow, srow, qrow, tag="a", bf_scratch=sqb, ones_bf=ones_b)
    b.barrier()
    rbw_2126 = b.sbuf("rbw2126", [1, T], BF16)
    emit_rstd(b, srow, qrow, T, 1.0 / D, 1e-5, rrow, srow, qrow, tag="a", rstd_bf=rbw_2126)
    b.barrier()
    ap = b.sem("ap")
    for (c0, w) in col_tiles(T):
        emit_bcast(b, psb1, rbw_2126[0:1, c0:c0 + w], w, onesrb, tag="a")
        bc = emit_bcast(b, psb2, srow[0:1, c0:c0 + w], w, ones_row, tag="a")
        nc.vector.wait_ge(bc.h, bc.n)
        for c in range(8):
            g = c * T + c0
            ap.inc(nc.vector.tensor_sub(out=scr[:, 0:w], in0=xg[:, g:g + w],
                                        in1=psb2[:, 0:w]))
            ap.inc(nc.vector.tensor_mul(out=n3[:, g:g + w], in0=scr[:, 0:w],
                                        in1=psb1[:, 0:w]))
        nc.tensor.wait_ge(ap.h, ap.n)
    b.barrier()

    mm = b.sem("mm")
    pr = b.sem("pr")
    for qk in range(2):
        dsto = qto if qk == 0 else kto
        for hc in range(8):
            oc = qk * 8 + hc
            k = 0
            for (c0, w) in col_tiles(T):
                ps = banks[k % 3]
                if k >= 3:
                    nc.tensor.wait_ge(pr.h, pr.n)
                ins = None
                for ic in range(8):
                    ins = nc.tensor.matmul(
                        ps[:, 0:w],
                        wt[:, ic * 3 * D + oc * 128: ic * 3 * D + oc * 128 + 128],
                        n3[:, ic * T + c0: ic * T + c0 + w],
                        start=(ic == 0), stop=(ic == 7))
                mm.inc(ins)
                nc.vector.wait_ge(mm.h, mm.n)
                pr.inc(nc.vector.tensor_copy(out=F1[:, c0:c0 + w], in_=ps[:, 0:w]))
                k += 1
            nc.vector.wait_ge(pr.h, pr.n)
            nc.scalar.wait_ge(pr.h, pr.n)
            pr.inc(nc.scalar.activation(out=sq[:, 0:512], in_=F1[:, 0:512],
                                        func=AF.Square))
            for ti2, (c0, w) in enumerate(col_tiles(T)):
                if ti2 > 0:
                    nc.scalar.wait_ge(pr.h, pr.n)
                    pr.inc(nc.scalar.activation(out=sq[:, 0:512],
                                                in_=F1[:, c0:c0 + w],
                                                func=AF.Square))
                nc.tensor.wait_ge(pr.h, pr.n)
                pr.inc(nc.tensor.matmul(psA[0:1, 0:w], onesf[0:64, :],
                                        sq[0:64, 0:w], start=True, stop=True))
                pr.inc(nc.tensor.matmul(psB[0:1, 0:w], onesf[64:128, :],
                                        sq[64:128, 0:w], start=True, stop=True))
                nc.vector.wait_ge(pr.h, pr.n)
                pr.inc(nc.vector.tensor_copy(out=nrmA[0:1, c0:c0 + w],
                                             in_=psA[0:1, 0:w]))
                pr.inc(nc.vector.tensor_copy(out=nrmB[0:1, c0:c0 + w],
                                             in_=psB[0:1, 0:w]))
                nc.tensor.wait_ge(pr.h, pr.n)
            for g2, nr in ((0, nrmA), (1, nrmB)):
                col = qk * 32 + hc * 2 + g2
                pr.inc(nc.vector.tensor_reduce(
                    out=nb[0:1, col:col + 1], in_=nr[0:1, :], axis=AX, op=ALU.max))
                nc.vector.wait_ge(pr.h, pr.n)
            pr.inc(nc.vector.tensor_copy(out=B1[:], in_=F1[:]))
            nc.vector.wait_ge(pr.h, pr.n)
            pr.inc(nc.vector.tensor_copy(out=B2[0:32, :], in_=B1[32:64, :]))
            pr.inc(nc.vector.tensor_copy(out=B2[32:64, :], in_=B1[0:32, :]))
            pr.inc(nc.vector.tensor_copy(out=B2[64:96, :], in_=B1[96:128, :]))
            pr.inc(nc.vector.tensor_copy(out=B2[96:128, :], in_=B1[64:96, :]))
            nc.vector.wait_ge(pr.h, pr.n)
            pr.inc(nc.vector.tensor_mul(out=F1[:], in0=B2[:], in1=tBb[:]))
            nc.vector.wait_ge(pr.h, pr.n)
            for half in range(2):
                h0 = half * 1024
                pr.inc(nc.vector.tensor_mul(out=sq[:, 0:1024],
                                            in0=B1[:, h0:h0 + 1024],
                                            in1=tAb[:, h0:h0 + 1024]))
                nc.vector.wait_ge(pr.h, pr.n)
                pr.inc(nc.vector.tensor_add(out=QK[:, h0:h0 + 1024],
                                            in0=sq[:, 0:1024],
                                            in1=F1[:, h0:h0 + 1024]))
                nc.vector.wait_ge(pr.h, pr.n)
            nc.gpsimd.wait_ge(pr.h, pr.n)
            b.store(dsto[hc * 128:(hc + 1) * 128, :], QK[:], sem=b.sem("st"))
    mmv = b.sem("mmv")
    evv = b.sem("evv")
    st = b.sem("st")
    stv = b.sem("stv")
    for tt in range(16):
        i = tt % 2
        if stv.n >= 32:
            nc.vector.wait_ge(stv.h, stv.n - 16)
        evv.inc(nc.vector.memset(vt[i][:], 1.0))
        for half in range(2):
            ps = banks[(tt * 2 + half) % 3]
            if tt * 2 + half >= 3:
                nc.tensor.wait_ge(evv.h, evv.n - 3)
            ins = None
            for ic in range(8):
                ins = nc.tensor.matmul(
                    ps[:, 0:512],
                    n3[:, ic * T + tt * 128: ic * T + tt * 128 + 128],
                    wt[:, ic * 3 * D + 2048 + half * 512: ic * 3 * D + 2048 + half * 512 + 512],
                    start=(ic == 0), stop=(ic == 7))
            mmv.inc(ins)
            nc.vector.wait_ge(mmv.h, mmv.n)
            evv.inc(nc.vector.tensor_copy(
                out=vt[i][:, half * 520: half * 520 + 520]
                    .rearrange("p (h d) -> p h d", h=8)[:, :, 0:64],
                in_=ps[:, 0:512].rearrange("p (h d) -> p h d", h=8)))
        nc.gpsimd.wait_ge(evv.h, evv.n)
        b.store(vto[tt * 128:(tt + 1) * 128, :], vt[i][:], sem=stv)
    b.store(nbo, nb[:], sem=st)
    nc.gpsimd.wait_ge(st.h, st.n)
    nc.gpsimd.wait_ge(stv.h, stv.n)
    return


def emit_L14(b, A):
    S = 8192
    nc = b.nc
    qi_ = A["QT3"]
    kfi = A["KTg"]
    vfi = A["Vg"]
    nbq = A["nbq"]
    nbk = A["nbks"]
    oo = A["obf"]

    qz = [b.sbuf(f"qz{i}", [128, 2 * T], BF16) for i in range(2)]
    ktc = [b.sbuf(f"ktc{i}", [128, S], BF16) for i in range(2)]
    vts = [b.sbuf(f"vts{i}", [128, 64 * 65], BF16) for i in range(2)]
    eb = b.sbuf("eb", [128, 64 * 512], BF16)
    obf = b.sbuf("obfs", [128, 8 * T], BF16)
    nbr = b.sbuf("nbr", [1, 64], F32)
    nkr = b.sbuf("nkr", [1, 256], F32)
    brow = b.sbuf("brow", [1, 64], F32)
    bcol = b.sbuf("bcol", [128, 64], F32)
    zr = b.sbuf("zr", [1, 512], F32)
    zrb = b.sbuf("zrb", [1, 512], BF16)
    rbs = [b.sbuf(f"rbs{i}", [64, 512], F32) for i in range(2)]
    ones_row = b.sbuf("ones_row", [1, 128], F32)
    onesrb = b.sbuf("onesrb", [1, 128], BF16)
    ones_rb = b.sbuf("ones_rb", [1, 64], BF16)
    pss = [b.psum(f"pss{i}", [128, 1024]) for i in range(2)]
    pso = [b.psum(f"pso{i}", [65, 512]) for i in range(2)]
    psz = [b.psum(f"psz{i}", [64, 512]) for i in range(2)]

    ld = b.sem("ld")
    ldk = b.sem("ldk")
    ldv = b.sem("ldv")
    ldq = b.sem("ldq")

    def load_qz(hc):
        i = hc % 2
        b.load(qz[i][0:64, 0:T], qi_[hc * 128: hc * 128 + 64, :], sem=ldq)
        b.load(qz[i][64:128, T:2 * T],
               qi_[hc * 128 + 64: hc * 128 + 128, :], sem=ldq)
        return ldq.n

    def load_kt(hc):
        for slot in range(4):
            b.load(ktc[hc % 2][:, slot * 2048:(slot + 1) * 2048],
                   kfi[slot * 1024 + hc * 128: slot * 1024 + hc * 128 + 128, :],
                   sem=ldk)
        return ldk.n

    def load_v(h):
        i = h % 2
        for slot in range(4):
            b.load(vts[i][:, slot * 16 * 65:(slot + 1) * 16 * 65]
                   .rearrange("p (t c) -> p t c", t=16),
                   vfi[slot * 2048:(slot + 1) * 2048, h * 65:(h + 1) * 65]
                   .rearrange("(t p) c -> p t c", p=128), sem=ldv)
        return ldv.n

    qzz = b.sem("qzz")
    qzz.inc(nc.vector.memset(qz[0][:], 0.0))
    qzz.inc(nc.vector.memset(qz[1][:], 0.0))
    nc.sync.wait_ge(qzz.h, qzz.n)
    qmarks = [load_qz(0)]
    b.load(nbr[:], nbq, sem=ld)
    nc.sync.wait_ge(b.sem("ccgn").h, 1)
    b.load(nkr[:], nbk, sem=ld)
    nc.sync.wait_ge(b.sem("ccgk").h, 1)
    kmarks = [load_kt(0)]
    nc.sync.wait_ge(b.sem("ccgv").h, 1)
    vmarks = [load_v(0)]
    nc.vector.wait_ge(ld.h, ld.n)
    nc.vector.memset(ones_row[:], 1.0)
    nc.vector.memset(onesrb[:], 1.0)
    nc.vector.memset(ones_rb[:], 1.0)
    nc.vector.memset(brow[:], 0.0)
    b.prep_consts([1e-30, -1.0])
    pr = b.sem("pr")
    for h in range(16):
        pr.inc(nc.vector.tensor_reduce(
            out=brow[0:1, 32 + h:33 + h],
            in_=nkr[0:1, 32 + h:256:64], axis=AX, op=ALU.max))
        nc.vector.wait_ge(pr.h, pr.n)
    pr.inc(nc.vector.tensor_mul(out=brow[0:1, 0:16], in0=nbr[0:1, 0:16],
                                in1=brow[0:1, 32:48]))
    nc.scalar.wait_ge(pr.h, pr.n)
    pr.inc(nc.scalar.activation(out=brow[0:1, 0:16], in_=brow[0:1, 0:16],
                                func=AF.Sqrt))
    nc.vector.wait_ge(pr.h, pr.n)
    pr.inc(nc.vector.tensor_scalar_mul(out=brow[0:1, 0:16],
                                       in0=brow[0:1, 0:16], scalar1=-0.125))
    nc.vector.wait_ge(pr.h, pr.n)
    nc.tensor.wait_ge(pr.h, pr.n)
    pr.inc(nc.tensor.matmul(psz[0][0:64, 0:64], ones_row[0:1, 0:64],
                            brow[0:1, 0:64], start=True, stop=True))
    nc.vector.wait_ge(pr.h, pr.n)
    pr.inc(nc.vector.tensor_copy(out=bcol[0:64, :], in_=psz[0][0:64, 0:64]))
    pr.inc(nc.vector.tensor_copy(out=bcol[64:128, :], in_=psz[0][0:64, 0:64]))
    nc.vector.wait_ge(pr.h, pr.n)
    b.barrier()

    mm = b.sem("mm")
    ex = b.sem("ex")
    av = b.sem("av")
    no = b.sem("no")
    NG = 32
    ex_hist = []
    gidx = 0
    steps = [(hc, h2, qi) for hc in range(8) for h2 in range(2)
             for qi in range(4)]
    ex_marks = {}
    av_marks = {}
    no_marks = {}

    def emit_av_pair(ps_, g):
        phc, ph2, pqi = steps[ps_]
        ph = phc * 2 + ph2
        ins = None
        for j in range(2):
            kc = g * 2 + j
            ins = nc.tensor.matmul(
                pso[ps_ % 2][:, 0:512],
                vts[ph % 2][:, kc * 65:(kc + 1) * 65],
                eb[:, kc * 512:(kc + 1) * 512],
                start=(kc == 0), stop=(kc == 63))
        av.inc(ins)
        av_marks[ps_].append(av.n)

    def emit_norm(ps_):
        phc, ph2, pqi = steps[ps_]
        ph = phc * 2 + ph2
        r0p = 64 * ph2
        qs0p = phc * T + pqi * 512
        ob = pso[ps_ % 2]
        pz = psz[ps_ % 2]
        nc.vector.wait_ge(av.h, av_marks[ps_][-1])
        no.inc(nc.vector.tensor_copy(out=zr[:], in_=ob[64:65, 0:512]))
        nc.vector.wait_ge(no.h, no.n)
        no.inc(nc.vector.tensor_scalar_max(out=zr[:], in0=zr[:],
                                           scalar1=1e-30))
        nc.scalar.wait_ge(no.h, no.n)
        no.inc(nc.scalar.activation(out=zr[:], in_=zr[:], func=AF.Ln))
        nc.scalar.wait_ge(no.h, no.n)
        no.inc(nc.scalar.activation(out=zrb[:], in_=zr[:],
                                    func=AF.Exp, scale=-1.0))
        nc.tensor.wait_ge(no.h, no.n)
        no.inc(nc.tensor.matmul(pz[0:64, 0:512], ones_rb[0:1, 0:64],
                                zrb[:], start=True, stop=True))
        nc.vector.wait_ge(no.h, no.n)
        no.inc(nc.vector.tensor_copy(out=rbs[ps_ % 2][:],
                                     in_=pz[0:64, 0:512]))
        nc.vector.wait_ge(no.h, no.n)
        no.inc(nc.vector.tensor_mul(
            out=obf[r0p:r0p + 64, qs0p: qs0p + 512],
            in0=ob[0:64, 0:512], in1=rbs[ps_ % 2][:]))
        no_marks[ps_] = no.n

    for s, (hc, h2, qi) in enumerate(steps):
        h = hc * 2 + h2
        ex_marks[s] = []
        if s >= 1:
            av_marks[s - 1] = []
        if h2 == 0 and qi == 0:
            nc.tensor.wait_ge(ldk.h, kmarks[hc])
            nc.tensor.wait_ge(ldq.h, qmarks[hc])
            if hc + 1 < 8:
                nc.sync.wait_ge(mm.h, mm.n)
                kmarks.append(load_kt(hc + 1))
                qmarks.append(load_qz(hc + 1))
        if qi == 0:
            nc.tensor.wait_ge(ldv.h, vmarks[h])
        for g in range(NG):
            if gidx >= 2:
                nc.tensor.wait_ge(ex.h, ex_hist[gidx - 2])
            for j in range(2):
                kc = g * 2 + j
                mm.inc(nc.tensor.matmul(
                    pss[g % 2][:, j * 512:(j + 1) * 512],
                    ktc[hc % 2][:, kc * 128: kc * 128 + 128],
                    qz[hc % 2][:, h2 * T + qi * 512:
                               h2 * T + qi * 512 + 512],
                    start=True, stop=True))
            gidx += 1
            if s >= 1:
                nc.tensor.wait_ge(ex.h, ex_marks[s - 1][g])
                if g == 0 and s >= 3:
                    nc.tensor.wait_ge(no.h, no_marks[s - 3])
                emit_av_pair(s - 1, g)
            nc.scalar.wait_ge(mm.h, mm.n)
            if s >= 1:
                nc.scalar.wait_ge(av.h, av_marks[s - 1][g])
            ex.inc(nc.scalar.activation(
                out=eb[:, g * 1024:(g + 1) * 1024],
                in_=pss[g % 2][:, 0:1024], func=AF.Exp,
                bias=bcol[:, h:h + 1], scale=0.125))
            ex_hist.append(ex.n)
            ex_marks[s].append(ex.n)
        if s >= 1:
            emit_norm(s - 1)
        if qi == 0 and h + 1 < 16:
            if s >= 1:
                nc.sync.wait_ge(av.h, av_marks[s - 1][-1])
            vmarks.append(load_v(h + 1))
    s_last = len(steps) - 1
    av_marks[s_last] = []
    nc.tensor.wait_ge(ex.h, ex_marks[s_last][-1])
    nc.tensor.wait_ge(no.h, no_marks[s_last - 2])
    for g in range(NG):
        emit_av_pair(s_last, g)
    emit_norm(s_last)
    b.barrier()
    st = b.sem("st")
    for c in range(8):
        b.store(oo[c * 128:(c + 1) * 128, :], obf[:, c * T:(c + 1) * T], sem=st)
    nc.gpsimd.wait_ge(st.h, st.n)
    return


def emit_L14b(b, A):
    nc = b.nc
    ti = A["obf"]
    x2i = A["x2"]
    wd = A["wo3"]
    x3o = A["x3"]

    tt = b.sbuf("tts", [128, 8 * T], BF16)
    wt = b.sbuf("wt", [128, 8 * D], BF16)
    a1 = b.sbuf("a1", [128, 8 * T], F32)
    x0t = [b.sbuf(f"x0t{i}", [128, 512], F32) for i in range(2)]
    x1t = [b.sbuf(f"x1t{i}", [128, 512], F32) for i in range(2)]
    ones_col = b.sbuf("ones_col", [128, 1], F32)
    ones_row = b.sbuf("ones_row", [1, 128], F32)
    onesrb = b.sbuf("onesrb", [1, 128], BF16)
    sq = b.sbuf("sq", [128, 1024], F32)
    scr = sq[:, 512:1024]
    srow = b.sbuf("srow", [1, T], F32)
    qrow = b.sbuf("qrow", [1, T], F32)
    rrow = b.sbuf("rrow", [1, T], F32)
    psrow = b.psum("psrow", [1, 1024])
    psb1 = b.psum("psb1", [128, 512])
    psb2 = b.psum("psb2", [128, 512])
    banks = [b.psum(f"pb{i}", [128, 512]) for i in range(4)]

    ld = b.sem("ld")
    for c in range(8):
        b.load(tt[:, c * T:(c + 1) * T], ti[c * 128:(c + 1) * 128, :], sem=ld)
    load_weight_bf16(b, wt, wd, 8, D)
    nc.vector.wait_ge(ld.h, ld.n)
    nc.vector.memset(ones_col[:], 1.0)
    nc.vector.memset(ones_row[:], 1.0)
    nc.vector.memset(onesrb[:], 1.0)
    b.prep_consts([1e-5])
    b.barrier()

    def evac(oc, c0, w, ps, mm, ev):
        nc.vector.wait_ge(mm.h, mm.n)
        ev.inc(nc.vector.tensor_copy(
            out=a1[:, oc * T + c0:oc * T + c0 + w], in_=ps[:, 0:w]))

    emit_proj(b, wt, D, 8, 8,
              lambda ic, c0, w: tt[:, ic * T + c0:ic * T + c0 + w],
              evac, col_tiles(T), banks)
    b.barrier()
    sqb = b.sbuf("sqb", [128, 1024], BF16)
    ones_b = b.sbuf("ones_b", [128, 1], BF16)
    nc.vector.memset(ones_b[:], 1.0)
    b.barrier()
    emit_stats(b, lambda c: a1[:, c * T:(c + 1) * T], 8, T, ones_col, sq,
               psrow, srow, qrow, tag="f", bf_scratch=sqb, ones_bf=ones_b)
    b.barrier()
    rbw_2519 = b.sbuf("rbw2519", [1, T], BF16)
    mbw_2519 = b.sbuf("mbw2519", [1, T], BF16)
    emit_rstd(b, srow, qrow, T, 1.0 / D, 1e-5, rrow, srow, qrow, tag="f", rstd_bf=rbw_2519, mu_bf=mbw_2519)
    b.barrier()
    ap = b.sem("ap")
    st = b.sem("st")
    ld2 = b.sem("ld2")
    ap_marks = []
    for (c0, w) in col_tiles(T):
        emit_bcast(b, psb1, rbw_2519[0:1, c0:c0 + w], w, onesrb, tag="f")
        bc = emit_bcast(b, psb2, mbw_2519[0:1, c0:c0 + w], w, onesrb, tag="f")
        nc.vector.wait_ge(bc.h, bc.n)
        for c in range(8):
            i = c % 2
            if len(ap_marks) >= 2:
                nc.sync.wait_ge(ap.h, ap_marks[-2])
            b.load(x0t[i][:, 0:w], x2i[c * 128:(c + 1) * 128, c0:c0 + w],
                   sem=ld2, eng=nc.sync)
            nc.vector.wait_ge(ld2.h, ld2.n)
            if st.n >= 32:
                nc.vector.wait_ge(st.h, st.n - 16)
            g = c * T + c0
            ap.inc(nc.vector.tensor_sub(out=scr[:, 0:w], in0=a1[:, g:g + w],
                                        in1=psb2[:, 0:w]))
            ap.inc(nc.vector.tensor_mul(out=x1t[i][:, 0:w], in0=scr[:, 0:w],
                                        in1=psb1[:, 0:w]))
            ap.inc(nc.vector.tensor_add(out=x1t[i][:, 0:w], in0=x1t[i][:, 0:w],
                                        in1=x0t[i][:, 0:w]))
            ap_marks.append(ap.n)
            nc.gpsimd.wait_ge(ap.h, ap.n)
            b.store(x3o[c * 128:(c + 1) * 128, c0:c0 + w], x1t[i][:, 0:w], sem=st)
        nc.tensor.wait_ge(ap.h, ap.n)
    nc.gpsimd.wait_ge(st.h, st.n)
    return


def emit_L15a(b, A):
    nc = b.nc
    xi = A["x3"]
    cti = A["cT"]
    wri = A["wrT"]
    n4o = A["n4"]
    gto = A["gT"]

    xg = b.sbuf("xg", [128, 8 * T], F32)
    n4 = b.sbuf("n4s", [128, 8 * T], BF16)
    ct = b.sbuf("ct", [128, 64], F32)
    cnb = b.sbuf("cnb", [128, 64], F32)
    wrv = b.sbuf("wrv", [128, 64], F32)
    csum = b.sbuf("csum", [1, 16], F32)
    scnc = b.sbuf("scnc", [8, 64], F32)
    swrc = b.sbuf("swrc", [8, 64], F32)
    mub = b.sbuf("mub", [8, 512], F32)
    lgc = b.sbuf("lgc", [8, T], F32)
    lgr = b.sbuf("lgr", [8, T], F32)
    glb = b.sbuf("glb", [128, 128], F32)
    gtm = b.sbuf("gtm", [128, 128], F32)
    gT = b.sbuf("gTs", [8, T], F32)
    crow = b.sbuf("crow", [1, 64], F32)
    crow2 = b.sbuf("crow2", [1, 64], F32)
    rnb = b.sbuf("rnb", [128, 64], F32)
    m1 = b.sbuf("m1", [128, 1], F32)
    m2 = b.sbuf("m2", [128, 1], F32)
    cnt = b.sbuf("cnt", [128, 1], F32)
    c2 = b.sbuf("c2", [128, 1], F32)
    thr = b.sbuf("thr", [128, 1], F32)
    t1c = b.sbuf("t1c", [128, 1], F32)
    eqb = b.sbuf("eqb", [128, 8], F32)
    gl2 = b.sbuf("gl2", [128, 8], F32)
    ebuf = b.sbuf("ebuf", [128, 8], F32)
    eye8 = b.sbuf("eye8", [8, 8], F32)
    eye128 = b.sbuf("eye128", [128, 128], F32)
    ones_col = b.sbuf("ones_col", [128, 1], F32)
    ones_row = b.sbuf("ones_row", [1, 128], F32)
    onesrb = b.sbuf("onesrb", [1, 128], BF16)
    sq = b.sbuf("sq", [128, 1024], F32)
    scr = sq[:, 512:1024]
    srow = b.sbuf("srow", [1, T], F32)
    qrow = b.sbuf("qrow", [1, T], F32)
    rrow = b.sbuf("rrow", [1, T], F32)
    nxr = b.sbuf("nxr", [1, T], F32)
    psrow = b.psum("psrow", [1, 512])
    psb1 = b.psum("psb1", [128, 512])
    psb2 = b.psum("psb2", [128, 512])
    psl = b.psum("psl", [8, 512])
    psl2 = b.psum("psl2", [8, 512])
    psgr = b.psum("psgr", [8, 512])
    psg = b.psum("psg", [128, 64])
    pst = b.psum("pst", [8, 128])

    e8c = nc.inline_tensor(np.ascontiguousarray(np.eye(8, dtype=np.float32)), b.pfx + "e8c")
    e128c = nc.inline_tensor(np.ascontiguousarray(np.eye(128, dtype=np.float32)), b.pfx + "e128c")

    ld = b.sem("ld")
    for c in range(8):
        b.load(xg[:, c * T:(c + 1) * T], xi[c * 128:(c + 1) * 128, :], sem=ld)
        b.load(ct[:, c * 8:(c + 1) * 8], cti[c * 128:(c + 1) * 128, :], sem=ld)
        b.load(wrv[:, c * 8:(c + 1) * 8], wri[c * 128:(c + 1) * 128, :], sem=ld)
    b.load(eye8[:], e8c.ap(), sem=ld)
    b.load(eye128[:], e128c.ap(), sem=ld)
    nc.vector.wait_ge(ld.h, ld.n)
    nc.vector.memset(ones_col[:], 1.0)
    nc.vector.memset(ones_row[:], 1.0)
    nc.vector.memset(onesrb[:], 1.0)
    b.prep_consts([1e-5])
    b.barrier()

    emit_stats(b, lambda c: xg[:, c * T:(c + 1) * T], 8, T, ones_col, sq,
               psrow, srow, qrow, tag="a")
    b.barrier()
    emit_rstd(b, srow, qrow, T, 1.0 / D, 1e-5, rrow, srow, nxr, tag="a")
    b.barrier()
    ap = b.sem("ap")
    for (c0, w) in col_tiles(T):
        emit_bcast(b, psb1, rrow[0:1, c0:c0 + w], w, ones_row, tag="a")
        bc = emit_bcast(b, psb2, srow[0:1, c0:c0 + w], w, ones_row, tag="a")
        nc.vector.wait_ge(bc.h, bc.n)
        for c in range(8):
            g = c * T + c0
            ap.inc(nc.vector.tensor_sub(out=scr[:, 0:w], in0=xg[:, g:g + w],
                                        in1=psb2[:, 0:w]))
            ap.inc(nc.vector.tensor_mul(out=n4[:, g:g + w], in0=scr[:, 0:w],
                                        in1=psb1[:, 0:w]))
        nc.tensor.wait_ge(ap.h, ap.n)
    b.barrier()

    cn = b.sem("cn")
    cn.inc(nc.scalar.activation(out=sq[:, 0:64], in_=ct[:], func=AF.Square))
    nc.tensor.wait_ge(cn.h, cn.n)
    cn.inc(nc.tensor.matmul(psrow[0:1, 0:64], ones_col[:], sq[:, 0:64],
                            start=True, stop=True))
    nc.vector.wait_ge(cn.h, cn.n)
    cn.inc(nc.vector.tensor_copy(out=crow[:], in_=psrow[0:1, 0:64]))
    nc.vector.wait_ge(cn.h, cn.n)
    for e in range(8):
        cn.inc(nc.vector.tensor_reduce(out=crow2[0:1, e:e + 1],
                                       in_=crow[0:1, e:64:8], axis=AX,
                                       op=ALU.add))
        nc.vector.wait_ge(cn.h, cn.n)
    nc.scalar.wait_ge(cn.h, cn.n)
    cn.inc(nc.scalar.activation(out=crow2[0:1, 0:8], in_=crow2[0:1, 0:8],
                                func=AF.Sqrt))
    nc.vector.wait_ge(cn.h, cn.n)
    cn.inc(nc.vector.tensor_scalar_max(out=crow2[0:1, 0:8],
                                       in0=crow2[0:1, 0:8], scalar1=1e-12))
    nc.vector.wait_ge(cn.h, cn.n)
    cn.inc(nc.vector.reciprocal(out=crow2[0:1, 0:8], in_=crow2[0:1, 0:8]))
    nc.vector.wait_ge(cn.h, cn.n)
    for c in range(8):
        cn.inc(nc.vector.tensor_copy(out=crow[0:1, c * 8:(c + 1) * 8],
                                     in_=crow2[0:1, 0:8]))
        nc.vector.wait_ge(cn.h, cn.n)
    nc.tensor.wait_ge(cn.h, cn.n)
    cn.inc(nc.tensor.matmul(psg[:, 0:64], ones_row[0:1, :], crow[0:1, 0:64],
                            start=True, stop=True))
    nc.vector.wait_ge(cn.h, cn.n)
    cn.inc(nc.vector.tensor_copy(out=rnb[:], in_=psg[:, 0:64]))
    nc.vector.wait_ge(cn.h, cn.n)
    cn.inc(nc.vector.tensor_mul(out=cnb[:], in0=ct[:], in1=rnb[:]))
    nc.vector.wait_ge(cn.h, cn.n)
    nc.tensor.wait_ge(cn.h, cn.n)
    cn.inc(nc.tensor.matmul(psrow[0:1, 0:64], ones_col[:], cnb[:],
                            start=True, stop=True))
    nc.vector.wait_ge(cn.h, cn.n)
    cn.inc(nc.vector.tensor_copy(out=crow[:], in_=psrow[0:1, 0:64]))
    nc.vector.wait_ge(cn.h, cn.n)
    for e in range(8):
        cn.inc(nc.vector.tensor_reduce(out=csum[0:1, e:e + 1],
                                       in_=crow[0:1, e:64:8], axis=AX,
                                       op=ALU.add))
        nc.vector.wait_ge(cn.h, cn.n)
    nc.tensor.wait_ge(cn.h, cn.n)
    cn.inc(nc.tensor.matmul(psrow[0:1, 0:64], ones_col[:], wrv[:],
                            start=True, stop=True))
    nc.vector.wait_ge(cn.h, cn.n)
    cn.inc(nc.vector.tensor_copy(out=crow[:], in_=psrow[0:1, 0:64]))
    nc.vector.wait_ge(cn.h, cn.n)
    for e in range(8):
        cn.inc(nc.vector.tensor_reduce(out=csum[0:1, 8 + e:9 + e],
                                       in_=crow[0:1, e:64:8], axis=AX,
                                       op=ALU.add))
        nc.vector.wait_ge(cn.h, cn.n)
    nc.tensor.wait_ge(cn.h, cn.n)
    cn.inc(nc.tensor.matmul(pst[0:8, 0:64], csum[0:1, 0:8], ones_row[0:1, 0:64],
                            start=True, stop=True))
    nc.vector.wait_ge(cn.h, cn.n)
    cn.inc(nc.vector.tensor_copy(out=scnc[:], in_=pst[0:8, 0:64]))
    nc.tensor.wait_ge(cn.h, cn.n)
    cn.inc(nc.tensor.matmul(pst[0:8, 0:64], csum[0:1, 8:16], ones_row[0:1, 0:64],
                            start=True, stop=True))
    nc.vector.wait_ge(cn.h, cn.n)
    cn.inc(nc.vector.tensor_copy(out=swrc[:], in_=pst[0:8, 0:64]))
    nc.vector.wait_ge(cn.h, cn.n)
    cn.inc(nc.vector.tensor_mul(out=nxr[:], in0=srow[:], in1=srow[:]))
    nc.vector.wait_ge(cn.h, cn.n)
    cn.inc(nc.vector.tensor_scalar(out=nxr[:], in0=nxr[:], scalar1=-1024.0,
                                   scalar2=None, op0=ALU.mult))
    nc.vector.wait_ge(cn.h, cn.n)
    cn.inc(nc.vector.tensor_add(out=nxr[:], in0=nxr[:], in1=qrow[:]))
    nc.vector.wait_ge(cn.h, cn.n)
    cn.inc(nc.vector.tensor_scalar_max(out=nxr[:], in0=nxr[:], scalar1=0.0))
    nc.scalar.wait_ge(cn.h, cn.n)
    cn.inc(nc.scalar.activation(out=nxr[:], in_=nxr[:], func=AF.Sqrt))
    nc.vector.wait_ge(cn.h, cn.n)
    cn.inc(nc.vector.tensor_mul(out=nxr[:], in0=nxr[:], in1=rrow[:]))
    nc.vector.wait_ge(cn.h, cn.n)
    cn.inc(nc.vector.tensor_scalar_max(out=nxr[:], in0=nxr[:], scalar1=1e-12))
    nc.vector.wait_ge(cn.h, cn.n)
    cn.inc(nc.vector.reciprocal(out=nxr[:], in_=nxr[:]))
    nc.vector.wait_ge(cn.h, cn.n)
    b.barrier()

    lg = b.sem("lg")
    for ti2, (c0, w) in enumerate(col_tiles(T)):
        ins1 = ins2 = None
        if ti2 > 0:
            nc.tensor.wait_ge(lg.h, lg.n)
        for c in range(8):
            ins1 = nc.tensor.matmul(psl[:, 0:w], cnb[:, c * 8:(c + 1) * 8],
                                    xg[:, c * T + c0: c * T + c0 + w],
                                    start=(c == 0), stop=(c == 7))
        for c in range(8):
            ins2 = nc.tensor.matmul(psl2[:, 0:w], wrv[:, c * 8:(c + 1) * 8],
                                    xg[:, c * T + c0: c * T + c0 + w],
                                    start=(c == 0), stop=(c == 7))
        lg.inc(ins1)
        lg.inc(ins2)
        nc.vector.wait_ge(lg.h, lg.n)
        lg.inc(nc.vector.tensor_copy(out=lgc[:, c0:c0 + w], in_=psl[:, 0:w]))
        lg.inc(nc.vector.tensor_copy(out=lgr[:, c0:c0 + w], in_=psl2[:, 0:w]))
        nc.tensor.wait_ge(lg.h, lg.n)
    b.barrier()
    for (c0, w) in col_tiles(T):
        lg.inc(nc.tensor.matmul(psl[:, 0:w], ones_row[0:1, 0:8],
                                srow[0:1, c0:c0 + w], start=True, stop=True))
        lg.inc(nc.tensor.matmul(psl2[:, 0:w], ones_row[0:1, 0:8],
                                nxr[0:1, c0:c0 + w], start=True, stop=True))
        lg.inc(nc.tensor.matmul(psgr[:, 0:w], ones_row[0:1, 0:8],
                                rrow[0:1, c0:c0 + w], start=True, stop=True))
        nc.vector.wait_ge(lg.h, lg.n)
        lg.inc(nc.vector.tensor_scalar(out=mub[:, 0:w], in0=psl[:, 0:w],
                                       scalar1=scnc[:, 0:1], scalar2=None,
                                       op0=ALU.mult))
        nc.vector.wait_ge(lg.h, lg.n)
        lg.inc(nc.vector.tensor_sub(out=lgc[:, c0:c0 + w], in0=lgc[:, c0:c0 + w],
                                    in1=mub[:, 0:w]))
        nc.vector.wait_ge(lg.h, lg.n)
        lg.inc(nc.vector.tensor_mul(out=lgc[:, c0:c0 + w], in0=lgc[:, c0:c0 + w],
                                    in1=psl2[:, 0:w]))
        nc.vector.wait_ge(lg.h, lg.n)
        lg.inc(nc.vector.tensor_scalar(out=mub[:, 0:w], in0=psl[:, 0:w],
                                       scalar1=swrc[:, 0:1], scalar2=None,
                                       op0=ALU.mult))
        nc.vector.wait_ge(lg.h, lg.n)
        lg.inc(nc.vector.tensor_sub(out=mub[:, 0:w], in0=lgr[:, c0:c0 + w],
                                    in1=mub[:, 0:w]))
        nc.vector.wait_ge(lg.h, lg.n)
        lg.inc(nc.vector.tensor_add(out=lgc[:, c0:c0 + w], in0=lgc[:, c0:c0 + w],
                                    in1=mub[:, 0:w]))
        nc.vector.wait_ge(lg.h, lg.n)
        lg.inc(nc.vector.tensor_mul(out=lgc[:, c0:c0 + w], in0=lgc[:, c0:c0 + w],
                                    in1=psgr[:, 0:w]))
        nc.vector.wait_ge(lg.h, lg.n)
        nc.tensor.wait_ge(lg.h, lg.n)
    b.barrier()

    tp = b.sem("tp")
    for half in range(2):
        if half > 0:
            nc.tensor.wait_ge(tp.h, tp.n)
        for t8 in range(8):
            tt = half * 8 + t8
            tp.inc(nc.tensor.matmul(psg[:, t8 * 8:(t8 + 1) * 8],
                                    lgc[:, tt * 128:(tt + 1) * 128], eye8[:],
                                    start=True, stop=True))
        nc.vector.wait_ge(tp.h, tp.n)
        tp.inc(nc.vector.tensor_copy(out=glb[:, half * 64:(half + 1) * 64],
                                     in_=psg[:, 0:64]))
        nc.vector.wait_ge(tp.h, tp.n)
    v = nc.vector
    s = nc.scalar
    for tt in range(16):
        g = glb[:, tt * 8:(tt + 1) * 8]

        def vs(instr):
            tp.inc(instr)
            nc.vector.wait_ge(tp.h, tp.n)
            nc.scalar.wait_ge(tp.h, tp.n)

        vs(v.tensor_reduce(out=m1[:], in_=g, axis=AX, op=ALU.max))
        vs(v.tensor_scalar(out=eqb[:], in0=g, scalar1=m1[:], scalar2=None,
                           op0=ALU.is_ge))
        vs(v.tensor_reduce(out=cnt[:], in_=eqb[:], axis=AX, op=ALU.add))
        vs(v.scalar_tensor_tensor(out=gl2[:], in0=eqb[:], scalar=-1e30, in1=g,
                                  op0=ALU.mult, op1=ALU.add))
        vs(v.tensor_reduce(out=m2[:], in_=gl2[:], axis=AX, op=ALU.max))
        vs(v.tensor_scalar(out=c2[:], in0=cnt[:], scalar1=2.0, scalar2=None,
                           op0=ALU.is_ge))
        vs(v.tensor_mul(out=thr[:], in0=c2[:], in1=m1[:]))
        vs(v.tensor_scalar(out=t1c[:], in0=c2[:], scalar1=-1.0, scalar2=1.0,
                           op0=ALU.mult, op1=ALU.add))
        vs(v.tensor_mul(out=t1c[:], in0=t1c[:], in1=m2[:]))
        vs(v.tensor_add(out=thr[:], in0=thr[:], in1=t1c[:]))
        vs(v.tensor_scalar(out=eqb[:], in0=g, scalar1=thr[:], scalar2=None,
                           op0=ALU.is_ge))
        vs(v.tensor_scalar(out=m1[:], in0=m1[:], scalar1=-1.0, scalar2=None,
                           op0=ALU.mult))
        vs(s.activation(out=ebuf[:], in_=g, func=AF.Exp, bias=m1[:]))
        vs(v.tensor_mul(out=ebuf[:], in0=ebuf[:], in1=eqb[:]))
        vs(v.tensor_reduce(out=cnt[:], in_=ebuf[:], axis=AX, op=ALU.add))
        vs(v.reciprocal(out=cnt[:], in_=cnt[:]))
        vs(v.tensor_scalar(out=gtm[:, tt * 8:(tt + 1) * 8], in0=ebuf[:],
                           scalar1=cnt[:], scalar2=None, op0=ALU.mult))
    b.barrier()
    tb2 = b.sem("tb2")
    for tt in range(16):
        nc.tensor.wait_ge(tb2.h, tb2.n)
        tb2.inc(nc.tensor.matmul(pst[:], gtm[:, tt * 8:(tt + 1) * 8], eye128[:],
                                 start=True, stop=True))
        nc.vector.wait_ge(tb2.h, tb2.n)
        tb2.inc(nc.vector.tensor_copy(out=gT[:, tt * 128:(tt + 1) * 128],
                                      in_=pst[:]))
    b.barrier()
    st = b.sem("st")
    for c in range(8):
        b.store(n4o[c * 128:(c + 1) * 128, :], n4[:, c * T:(c + 1) * T], sem=st)
    b.store(gto, gT[:], sem=st)
    nc.gpsimd.wait_ge(st.h, st.n)
    return


def emit_L15b(b, A):
    nc = b.nc
    n4i = A["n4"]
    gti = A["gT"]
    w1i = A["w1e"]
    b1i = A["b1e"]
    hbo = A["hb"]

    n4 = b.sbuf("n4s", [128, 8 * T], BF16)
    gT = b.sbuf("gTs", [8, T], F32)
    wt = b.sbuf("wt", [128, 64 * 512], BF16)
    b1v = b.sbuf("b1v", [128, 32], F32)
    ohs = b.sbuf("ohs", [8, 8 * 128], F32)
    hbt = [b.sbuf(f"hbt{i}", [128, 512], BF16) for i in range(3)]
    banks = [b.psum(f"pb{i}", [128, 512]) for i in range(3)]
    psgb = b.psum("psgb", [128, 512])

    oh_np = np.zeros((8, 8 * 128), np.float32)
    for e in range(8):
        oh_np[e, e * 128:(e + 1) * 128] = 1.0
    ohc = nc.inline_tensor(np.ascontiguousarray(oh_np), b.pfx + "ohc")

    ld = b.sem("ld")
    for c in range(8):
        b.load(n4[:, c * T:(c + 1) * T], n4i[c * 128:(c + 1) * 128, :], sem=ld)
    b.load(gT[:], gti, sem=ld)
    b.load(b1v[:], b1i, sem=ld)
    b.load(ohs[:], ohc.ap(), sem=ld)
    for e in range(8):
        for ic in range(8):
            b.load(wt[:, (e * 8 + ic) * 512:(e * 8 + ic) * 512 + 512],
                   w1i[e * D + ic * 128: e * D + ic * 128 + 128, :], sem=ld)
    nc.vector.wait_ge(ld.h, ld.n)
    b.barrier()

    mm = b.sem("mm")
    gb = b.sem("gb")
    ev = b.sem("ev")
    mu = b.sem("mu")
    st = b.sem("st")
    kk = 0
    for e in range(8):
        for ct_i, (c0, w) in enumerate(col_tiles(T)):
            if kk > 0:
                nc.tensor.wait_ge(mu.h, mu.n)
            gb.inc(nc.tensor.matmul(psgb[:, 0:w],
                                    ohs[:, e * 128:(e + 1) * 128],
                                    gT[:, c0:c0 + w], start=True, stop=True))
            for oc in range(4):
                ps = banks[kk % 3]
                if kk >= 3:
                    nc.tensor.wait_ge(ev.h, ev.n - 2)
                ins = None
                for ic in range(8):
                    ins = nc.tensor.matmul(
                        ps[:, 0:w],
                        wt[:, (e * 8 + ic) * 512 + oc * 128:
                           (e * 8 + ic) * 512 + oc * 128 + 128],
                        n4[:, ic * T + c0: ic * T + c0 + w],
                        start=(ic == 0), stop=(ic == 7))
                mm.inc(ins)
                nc.scalar.wait_ge(mm.h, mm.n)
                if mu.n > 2:
                    nc.scalar.wait_ge(mu.h, mu.n - 2)
                if st.n >= 16 * 3:
                    nc.scalar.wait_ge(st.h, st.n - 16 * 2)
                bc = e * 4 + oc
                ev.inc(nc.scalar.activation(
                    out=hbt[kk % 3][:, 0:w], in_=ps[:, 0:w], func=AF.Gelu,
                    bias=b1v[:, bc:bc + 1]))
                nc.vector.wait_ge(ev.h, ev.n)
                nc.vector.wait_ge(gb.h, gb.n)
                mu.inc(nc.vector.tensor_mul(out=hbt[kk % 3][:, 0:w],
                                            in0=hbt[kk % 3][:, 0:w],
                                            in1=psgb[:, 0:w]))
                nc.gpsimd.wait_ge(mu.h, mu.n)
                b.store(hbo[e * 512 + oc * 128: e * 512 + oc * 128 + 128,
                            c0:c0 + w], hbt[kk % 3][:, 0:w], sem=st)
                kk += 1
    nc.gpsimd.wait_ge(st.h, st.n)
    return


def emit_L15c(b, A):
    nc = b.nc
    hbi = A["hb"]
    gti = A["gT"]
    x3i = A["x3"]
    w2i = A["w2e"]
    b2i = A["b2e"]
    x4o = A["x4"]

    wt = b.sbuf("wt", [128, 32 * D], BF16)
    b2v = b.sbuf("b2v", [8, D], BF16)
    gT = b.sbuf("gTs", [8, T], F32)
    gTb = b.sbuf("gTb", [8, T], BF16)
    hbt = [b.sbuf("hbt0", [128, 32 * 512], BF16)]
    yv = b.sbuf("yv", [128, 8 * T], BF16)
    ones_bf = b.sbuf("ones_bf", [128, 1], BF16)
    x0t = [b.sbuf(f"x0t{i}", [128, 512], F32) for i in range(2)]
    x1t = [b.sbuf(f"x1t{i}", [128, 512], F32) for i in range(2)]
    ones_col = b.sbuf("ones_col", [128, 1], F32)
    ones_row = b.sbuf("ones_row", [1, 128], F32)
    onesrb = b.sbuf("onesrb", [1, 128], BF16)
    sq = b.sbuf("sq", [128, 1024], F32)
    scr = sq[:, 512:1024]
    srow = b.sbuf("srow", [1, T], F32)
    qrow = b.sbuf("qrow", [1, T], F32)
    rrow = b.sbuf("rrow", [1, T], F32)
    psrow = b.psum("psrow", [1, 1024])
    psb1 = b.psum("psb1", [128, 512])
    psb2 = b.psum("psb2", [128, 512])
    banks = [b.psum(f"pb{i}", [128, 512]) for i in range(4)]

    ld = b.sem("ld")
    ldt = b.sem("ldt")
    b.load(gT[:], gti, sem=ld)
    b.load(b2v[:], b2i, sem=ld)
    for e in range(8):
        for ic in range(4):
            b.load(wt[:, (e * 4 + ic) * D:(e * 4 + ic) * D + D],
                   w2i[e * 512 + ic * 128: e * 512 + ic * 128 + 128, :], sem=ld)

    def load_hct(c0):
        for j in range(32):
            b.load(hbt[0][:, j * 512:j * 512 + 512],
                   hbi[j * 128:(j + 1) * 128, c0:c0 + 512], sem=ldt)
        return ldt.n

    marks = [load_hct(0)]
    nc.vector.wait_ge(ld.h, ld.n)
    nc.vector.memset(ones_col[:], 1.0)
    nc.vector.memset(ones_row[:], 1.0)
    nc.vector.memset(onesrb[:], 1.0)
    nc.vector.memset(ones_bf[:], 1.0)
    nc.vector.tensor_copy(out=gTb[:], in_=gT[:])
    b.prep_consts([1e-5])
    b.barrier()

    mm = b.sem("mm")
    ev = b.sem("ev")
    tiles = col_tiles(T)
    for ct_i, (c0, w) in enumerate(tiles):
        if ct_i > 0:
            nc.sync.wait_ge(mm.h, mm.n)
            marks.append(load_hct(c0))
        nc.tensor.wait_ge(ldt.h, marks[ct_i])
        for oc in range(8):
            ps = banks[oc % 4]
            if ct_i * 8 + oc >= 4:
                nc.tensor.wait_ge(ev.h, ev.n - 3)
            nc.tensor.matmul(ps[:, 0:w], b2v[:, oc * 128:(oc + 1) * 128],
                             gTb[:, c0:c0 + w], start=True, stop=False)
            ins = None
            for j in range(32):
                ins = nc.tensor.matmul(
                    ps[:, 0:w],
                    wt[:, j * D + oc * 128: j * D + oc * 128 + 128],
                    hbt[0][:, j * 512:j * 512 + w],
                    start=False, stop=(j == 31))
            mm.inc(ins)
            nc.vector.wait_ge(mm.h, mm.n)
            ev.inc(nc.vector.tensor_copy(
                out=yv[:, oc * T + c0: oc * T + c0 + w], in_=ps[:, 0:w]))
    b.barrier()
    sqb = b.sbuf("sqb", [128, 1024], BF16)
    b.barrier()
    emit_stats(b, lambda c: yv[:, c * T:(c + 1) * T], 8, T, ones_bf, sqb,
               psrow, srow, qrow, tag="f", ones_sq=ones_bf)
    b.barrier()
    rbw_3032 = b.sbuf("rbw3032", [1, T], BF16)
    mbw_3032 = b.sbuf("mbw3032", [1, T], BF16)
    emit_rstd(b, srow, qrow, T, 1.0 / D, 1e-5, rrow, srow, qrow, tag="f", rstd_bf=rbw_3032, mu_bf=mbw_3032)
    b.barrier()
    ap = b.sem("ap")
    st = b.sem("st")
    ld2 = b.sem("ld2")
    ap_marks = []
    for (c0, w) in col_tiles(T):
        emit_bcast(b, psb1, rbw_3032[0:1, c0:c0 + w], w, onesrb, tag="f")
        bc = emit_bcast(b, psb2, mbw_3032[0:1, c0:c0 + w], w, onesrb, tag="f")
        nc.vector.wait_ge(bc.h, bc.n)
        for c in range(8):
            i = c % 2
            if len(ap_marks) >= 2:
                nc.sync.wait_ge(ap.h, ap_marks[-2])
            b.load(x0t[i][:, 0:w], x3i[c * 128:(c + 1) * 128, c0:c0 + w],
                   sem=ld2, eng=nc.sync)
            nc.vector.wait_ge(ld2.h, ld2.n)
            if st.n >= 32:
                nc.vector.wait_ge(st.h, st.n - 16)
            g = c * T + c0
            ap.inc(nc.vector.tensor_sub(out=scr[:, 0:w], in0=yv[:, g:g + w],
                                        in1=psb2[:, 0:w]))
            ap.inc(nc.vector.tensor_mul(out=x1t[i][:, 0:w], in0=scr[:, 0:w],
                                        in1=psb1[:, 0:w]))
            ap.inc(nc.vector.tensor_add(out=x1t[i][:, 0:w], in0=x1t[i][:, 0:w],
                                        in1=x0t[i][:, 0:w]))
            ap_marks.append(ap.n)
            nc.gpsimd.wait_ge(ap.h, ap.n)
            b.store(x4o[c * 128:(c + 1) * 128, c0:c0 + w], x1t[i][:, 0:w], sem=st)
        nc.tensor.wait_ge(ap.h, ap.n)
    nc.gpsimd.wait_ge(st.h, st.n)
    return


def _rope_tables(core):
    t0 = (core % 4) * T
    pos = np.arange(t0 + 1, t0 + T + 1, dtype=np.float32)
    half = np.arange(32, dtype=np.float32)
    thetas = 10000.0 ** (-2.0 * half / 64.0)
    ang = thetas[:, None] * pos[None, :]
    cos = np.cos(ang)
    sin = np.sin(ang)
    A = np.concatenate([cos, cos, cos, cos], axis=0)
    Bm = np.concatenate([-sin, sin, -sin, sin], axis=0)
    return nbf(A), nbf(Bm)


def _perm_qk_cols(w):
    w = np.asarray(w)
    idx = np.arange(64)
    perm = np.concatenate([idx[0::2], idx[1::2]])
    out = w.copy()
    for part in range(2):
        for h in range(16):
            base = part * D + h * 64
            out[:, base:base + 64] = w[:, base + perm]
    return out


def block3(d, x2_full):
    x2s = _shard_tokens(x2_full)
    wq3 = nbf(_perm_qk_cols(np.asarray(d["L1_attn_W_qkv"])))
    r13 = []
    tabs = [_rope_tables(c) for c in range(NCORES)]
    r13 = _run(build_L13(), [{"x2": x2s[c], "wqkv3": wq3, "tA3": tabs[c][0],
                              "tB3": tabs[c][1]} for c in range(NCORES)])
    out = []
    ktf = {}
    vh = {}
    nbks = {}
    for bb in range(2):
        cores = range(bb * 4, bb * 4 + 4)
        ktf[bb] = np.ascontiguousarray(
            np.concatenate([np.asarray(r13[c]["KT3"]) for c in cores], axis=1))
        vfull = np.concatenate([np.asarray(r13[c]["VT3"]) for c in cores], axis=0)
        vh[bb] = np.ascontiguousarray(
            np.stack([vfull[:, h * 65:(h + 1) * 65] for h in range(16)]))
        nbks[bb] = f32(np.concatenate(
            [np.asarray(r13[c]["nb3"]).reshape(64) for c in cores])[None, :])
    r14 = _run(build_L14(), [{"QT3": r13[c]["QT3"], "KTf": ktf[c // 4],
                              "Vh": vh[c // 4], "nbq": f32(r13[c]["nb3"]),
                              "nbks": nbks[c // 4]} for c in range(NCORES)])
    wo3 = nbf(np.asarray(d["L1_attn_W_out"]))
    r14b = _run(build_L14b(), [{"obf": r14[c]["obf"], "x2": x2s[c],
                                "wo3": wo3} for c in range(NCORES)])
    return [r_["x3"] for r_ in r14b]


def block4(d, x3s):
    cT = f32(np.asarray(d["L1_ffn_centroids"]).T)
    wrT = f32(np.asarray(d["L1_ffn_w_route"]).T)
    r15a = _run(build_L15a(), [{"x3": x3s[c], "cT": cT, "wrT": wrT}
                               for c in range(NCORES)])
    w1e = nbf(np.asarray(d["L1_ffn_w1"]).reshape(8 * D, 512))
    b1 = np.asarray(d["L1_ffn_b1"])
    b1c = np.zeros((128, 32), np.float32)
    for e in range(8):
        for oc in range(4):
            b1c[:, e * 4 + oc] = b1[e, oc * 128:(oc + 1) * 128]
    r15b = _run(build_L15b(), [{"n4": r15a[c]["n4"], "gT": r15a[c]["gT"],
                                "w1e": w1e, "b1e": f32(b1c)}
                               for c in range(NCORES)])
    w2e = nbf(np.asarray(d["L1_ffn_w2"]).reshape(8 * 512, D))
    b2e = nbf(np.asarray(d["L1_ffn_b2"]))
    r15c = _run(build_L15c(), [{"hb": r15b[c]["hb"], "gT": r15a[c]["gT"],
                                "x3": x3s[c], "w2e": w2e, "b2e": b2e}
                               for c in range(NCORES)])
    return [r_["x4"] for r_ in r15c]


def kernel(**inputs):
    x0 = np.asarray(inputs["L0_attn_x"], np.float32)
    x1s = block1(inputs, x0)
    x1_full = _assemble(x1s)
    x2s = block2(inputs, x1_full)
    x2_full = _assemble(x2s)
    x3s = block3(inputs, x2_full)
    x4s = block4(inputs, x3s)
    return _assemble(x4s)


def emit_dcopy(b, name, out_ap, in_ap):
    nc = b.nc
    s = b.sem("dc_" + name)
    s.inc(nc.gpsimd.dma_start(out=out_ap, in_=in_ap), 16)
    nc.gpsimd.wait_ge(s.h, s.n)
    b.barrier()


def emit_gather(b, name, in_ap, out_ap, groups):
    nc = b.nc
    s = b.sem("cc_" + name)
    s.inc(nc.gpsimd.collective_compute(
        "AllGather", ALU.bypass, replica_groups=groups,
        ins=[in_ap.opt()], outs=[out_ap.opt()]), 1)
    nc.gpsimd.wait_ge(s.h, 1)
    b.barrier()


def emit_halo(b, A, width, dtype):
    nc = b.nc
    gsrc = A["gsrc"]
    dst = A["dst"]
    hm = b.sbuf("hm", [128, 4], F32)
    sl = [b.sbuf(f"sl{i}", [128, width], dtype) for i in range(4)]
    acc = b.sbuf("acc", [128, width], dtype)
    ld = b.sem("ld")
    hs = b.sem("hs")
    st = b.sem("st")
    b.load(hm[:], A["hmask"], sem=ld)
    for c in range(8):
        for s in range(4):
            b.load(sl[s][:], gsrc[s * 1024 + c * 128: s * 1024 + c * 128 + 128, :],
                   sem=ld)
        nc.vector.wait_ge(ld.h, ld.n)
        hs.inc(nc.vector.tensor_scalar(out=acc[:], in0=sl[0][:],
                                       scalar1=hm[:, 0:1], scalar2=None,
                                       op0=ALU.mult))
        for s in range(1, 4):
            nc.vector.wait_ge(hs.h, hs.n)
            hs.inc(nc.vector.scalar_tensor_tensor(
                out=acc[:], in0=sl[s][:], scalar=hm[:, s:s + 1], in1=acc[:],
                op0=ALU.mult, op1=ALU.add))
        nc.gpsimd.wait_ge(hs.h, hs.n)
        b.store(dst[c * 128:(c + 1) * 128, :], acc[:], sem=st)
        nc.sync.wait_ge(st.h, st.n)
    nc.gpsimd.wait_ge(st.h, st.n)
    b.barrier()


GROUPS2 = [[0, 1, 2, 3], [4, 5, 6, 7]]
GROUPS1 = [[0, 1, 2, 3, 4, 5, 6, 7]]


def build_fused():
    b = B()
    nc = b.nc

    def inp(name, shape, dtype=F32):
        return b.inp(name, shape, dtype)

    def itn(name, shape, dtype=F32):
        return b.nc.dram_tensor(name, list(shape), dtype, kind="Internal").ap()

    E = {}
    E["x0g"] = inp("x0g", [D, T + 4])
    E["n1w"] = inp("n1w", [128, 8]); E["n2w"] = inp("n2w", [128, 8])
    E["wgt"] = inp("wgt", [D, HID], BF16); E["wit"] = inp("wit", [D, HID], BF16)
    E["convw"] = inp("convw", [128, 48]); E["convb"] = inp("convb", [128, 12])
    E["rgin_t"] = inp("rgin_t", [HID, HID], BF16)
    E["rggate_t"] = inp("rggate_t", [HID, HID], BF16)
    E["rginb"] = inp("rginb", [128, 12]); E["rggateb"] = inp("rggateb", [128, 12])
    E["lam"] = inp("lam", [128, 12]); E["cmask"] = inp("cmask", [128, 16])
    E["wot"] = inp("wot", [HID, D], BF16)
    E["mgt"] = inp("mgt", [D, HID], BF16); E["mut"] = inp("mut", [D, HID], BF16)
    E["mdt"] = inp("mdt", [HID, D], BF16)
    E["x0"] = inp("x0", [D, T])
    E["hmask"] = inp("hmask", [128, 4])
    for l in range(2):
        E[f"wqkv{l}"] = inp(f"wqkv{l}", [D, 1536], BF16)
        E[f"qs{l}"] = inp(f"qs{l}", [128, 1]); E[f"ks{l}"] = inp(f"ks{l}", [128, 1])
        E[f"w1_{l}"] = inp(f"w1_{l}", [D, 2 * HID2], BF16)
        E[f"b1c{l}"] = inp(f"b1c{l}", [128, 44])
        E[f"w2_{l}"] = inp(f"w2_{l}", [HID2, D], BF16)
        E[f"b2c{l}"] = inp(f"b2c{l}", [128, 8])
        E[f"lg{l}"] = inp(f"lg{l}", [128, 8]); E[f"lb{l}"] = inp(f"lb{l}", [128, 8])
        E[f"wo{l}"] = inp(f"wo{l}", [512, D], BF16)
    E["tAq"] = inp("tAq", [128, KV], BF16); E["tBq"] = inp("tBq", [128, KV], BF16)
    E["tAk"] = inp("tAk", [128, KV], BF16); E["tBk"] = inp("tBk", [128, KV], BF16)
    E["keep01"] = inp("keep01", [128, 21]); E["gmask"] = inp("gmask", [128, 1])
    E["wqkv3"] = inp("wqkv3", [D, 3 * D], BF16)
    E["tA3"] = inp("tA3", [128, T], BF16); E["tB3"] = inp("tB3", [128, T], BF16)
    E["wo3"] = inp("wo3", [D, D], BF16)
    E["cT"] = inp("cT", [D, 8]); E["wrT"] = inp("wrT", [D, 8])
    E["w1e"] = inp("w1e", [8 * D, 512], BF16)
    E["b1e"] = inp("b1e", [128, 32])
    E["w2e"] = inp("w2e", [8 * 512, D], BF16)
    E["b2e"] = inp("b2e", [8, D], BF16)
    x4o = b.out("x4", [D, T])

    pre1 = itn("i_pre1", [D, T + 4], BF16)
    n1 = itn("i_n1", [D, T + 4], BF16)
    gg = itn("i_gg", [HID, T], BF16)
    ucb = itn("i_ucb", [HID, T], BF16)
    sig = itn("i_sig", [HID, T], BF16)
    srg = itn("i_srg", [HID, T], BF16)
    hl = itn("i_hl", [HID, T], BF16)
    pp = itn("i_pp", [HID, T], BF16)
    pack = itn("i_pack", [128, 24])
    packg = b.nc.dram_tensor("i_packg", [8 * 128, 24], F32, kind="Internal",
                             addr_space="Shared").ap()
    bx = itn("i_bx", [D, T], BF16)
    n2 = itn("i_n2", [D, T], BF16)
    tt1 = itn("i_tt1", [HID, T], BF16)
    x1g_t = b.nc.dram_tensor("i_x1g", [D, WY], F32, kind="Internal")
    x1g = x1g_t.ap()
    x1tailg = b.nc.dram_tensor("i_x1tailg", [4 * D, HB], F32,
                               kind="Internal").ap()
    y0g_t = b.nc.dram_tensor("i_y0g", [D, WY], mybir.dt.bfloat16, kind="Internal")
    y0g = y0g_t.ap()
    y1g_t = b.nc.dram_tensor("i_y1g", [D, WY], mybir.dt.bfloat16, kind="Internal")
    y1g = y1g_t.ap()
    ytailg = b.nc.dram_tensor("i_ytailg", [4 * D, HB], mybir.dt.bfloat16,
                              kind="Internal").ap()
    QT = itn("i_QT", [512, KV], BF16)
    KT = itn("i_KT", [512, KV], BF16)
    VT = itn("i_VT", [KV, 520], BF16)
    nb = itn("i_nb", [1, 64])
    ym = itn("i_ym", [D, QEXT], BF16)
    tt2 = itn("i_tt2", [HID2, T], BF16)
    x2 = itn("i_x2", [D, T])
    QT3 = itn("i_QT3", [D, T], BF16)
    KT3 = itn("i_KT3", [D, T], BF16)
    VT3 = itn("i_VT3", [T, 1040], BF16)
    nb3 = itn("i_nb3", [1, 64])
    KTg = b.nc.dram_tensor("i_KTg", [4 * D, T], mybir.dt.bfloat16,
                           kind="Internal").ap()
    Vg = b.nc.dram_tensor("i_Vg", [4 * T, 1040], mybir.dt.bfloat16,
                          kind="Internal").ap()
    nbg_t = b.nc.dram_tensor("i_nbg", [4, 64], F32, kind="Internal")
    obf = itn("i_obf", [D, T], BF16)
    x3 = itn("i_x3", [D, T])
    n4 = itn("i_n4", [D, T], BF16)
    gT = itn("i_gT", [8, T])
    hb = itn("i_hb", [8 * 512, T], BF16)

    def stage(tag, fn, A, *args):
        b.push(tag)
        with nc.named_scope(tag):
            fn(b, A, *args)
        b.pop()
        b.barrier()

    stage("s01", emit_L01, {"x0g": E["x0g"], "n1w": E["n1w"],
                            "pre1": pre1, "n1": n1})
    stage("s02", emit_L02, {"n1": n1[:, 4:], "wgt": E["wgt"], "gg": gg})
    stage("s03", emit_L03, {"n1": n1, "wit": E["wit"], "convw": E["convw"],
                            "convb": E["convb"], "ucb": ucb})
    stage("s45", emit_L45, {"ucb": ucb, "rgin_t": E["rgin_t"],
                            "rggate_t": E["rggate_t"], "rginb": E["rginb"],
                            "rggateb": E["rggateb"], "sig": sig, "srg": srg})
    stage("s06", emit_L06, {"ucb": ucb, "sig": sig, "srg": srg, "lam": E["lam"],
                            "hl": hl, "pp": pp, "pack": pack})
    emit_gather(b, "pk", pack, packg, GROUPS1)
    stage("s07", emit_L07, {"hl": hl, "pp": pp, "gg": gg, "pre1": pre1[:, 4:],
                            "packs": packg, "cmask": E["cmask"], "wot": E["wot"],
                            "n2w": E["n2w"], "bx": bx, "n2": n2})
    stage("s08", emit_L08, {"n2": n2, "mgt": E["mgt"], "mut": E["mut"],
                            "tt": tt1})
    x1tail = itn("i_x1tail", [D, HB])
    stage("s09", emit_L09, {"tt": tt1, "bx": bx, "x0": E["x0"], "mdt": E["mdt"],
                            "x1": x1g[:, HB:], "x1tail": x1tail})
    b.sem("cc_x1t").inc(b.nc.gpsimd.collective_compute(
        "AllGather", ALU.bypass, replica_groups=GROUPS2,
        ins=[x1tail.opt()], outs=[x1tailg.opt()]), 1)
    b.nc.gpsimd.wait_ge(b.sem("cc_x1t").h, 1)
    b.barrier()
    b.push("h1")
    emit_halo(b, {"gsrc": x1tailg, "hmask": E["hmask"], "dst": x1g[:, 0:HB]},
              HB, F32)
    b.pop()
    stage("s09b", emit_L09b, {"x1g": x1g, "y0": y0g})
    ygs = [y0g, y1g]
    for l in range(2):
        yg = ygs[l]
        stage(f"s10_{l}", emit_L10,
              {"y0": yg, "wqkv": E[f"wqkv{l}"], "qs": E[f"qs{l}"],
               "ks": E[f"ks{l}"], "tAq": E["tAq"], "tBq": E["tBq"],
               "tAk": E["tAk"], "tBk": E["tBk"], "QT": QT, "KT": KT, "VT": VT,
               "nb": nb})
        stage(f"s11_{l}", emit_L11,
              {"QT": QT, "KT": KT, "VT": VT, "y0": yg, "keep01": E["keep01"],
               "gmask": E["gmask"], "nb": nb, "wo": E[f"wo{l}"], "ymid": ym})
        stage(f"s12a_{l}", emit_L12a,
              {"ymid": ym, "lg": E[f"lg{l}"], "lb": E[f"lb{l}"],
               "w1": E[f"w1_{l}"], "b1c": E[f"b1c{l}"], "tt": tt2})
        if l == 0:
            ytail = itn("i_ytail", [D, HB], BF16)
            stage(f"s12b_{l}", emit_L12b,
                  {"tt": tt2, "ymid": ym, "w2": E[f"w2_{l}"],
                   "b2c": E[f"b2c{l}"], "yout": y1g[:, HB:],
                   "ytail": ytail}, False)
            b.sem("cc_y1t").inc(b.nc.gpsimd.collective_compute(
                "AllGather", ALU.bypass, replica_groups=GROUPS2,
                ins=[ytail.opt()], outs=[ytailg.opt()]), 1)
            b.nc.gpsimd.wait_ge(b.sem("cc_y1t").h, 1)
            b.barrier()
            b.push("h2")
            emit_halo(b, {"gsrc": ytailg, "hmask": E["hmask"],
                          "dst": y1g[:, 0:HB]}, HB, BF16)
            b.pop()
        else:
            stage(f"s12b_{l}", emit_L12b,
                  {"tt": tt2, "ymid": ym, "w2": E[f"w2_{l}"],
                   "b2c": E[f"b2c{l}"], "x1": x1g[:, HB:], "x2": x2}, True)
    def _after_k(b2):
        b2.sem("ccgn").inc(b2.nc.gpsimd.collective_compute(
            "AllGather", ALU.bypass, replica_groups=GROUPS2,
            ins=[nb3.opt()], outs=[nbg_t.ap().opt()]), 1)
        b2.sem("ccgk").inc(b2.nc.gpsimd.collective_compute(
            "AllGather", ALU.bypass, replica_groups=GROUPS2,
            ins=[KT3.opt()], outs=[KTg.opt()]), 1)

    stage("s13", emit_L13, {"x2": x2, "wqkv3": E["wqkv3"], "tA3": E["tA3"],
                            "tB3": E["tB3"], "QT3": QT3, "KT3": KT3,
                            "VT3": VT3, "nb3": nb3})
    _after_k(b)
    b.sem("ccgv").inc(b.nc.gpsimd.collective_compute(
        "AllGather", ALU.bypass, replica_groups=GROUPS2,
        ins=[VT3.opt()], outs=[Vg.opt()]), 1)
    stage("s14", emit_L14, {"QT3": QT3, "KTg": KTg, "Vg": Vg, "nbq": nb3,
                            "nbks": nbg_t.reshape([1, 256]).ap(), "obf": obf})
    stage("s14b", emit_L14b, {"obf": obf, "x2": x2, "wo3": E["wo3"], "x3": x3})
    stage("s15a", emit_L15a, {"x3": x3, "cT": E["cT"], "wrT": E["wrT"],
                              "n4": n4, "gT": gT})
    stage("s15b", emit_L15b, {"n4": n4, "gT": gT, "w1e": E["w1e"],
                              "b1e": E["b1e"], "hb": hb})
    stage("s15c", emit_L15c, {"hb": hb, "gT": gT, "x3": x3, "w2e": E["w2e"],
                              "b2e": E["b2e"], "x4": x4o})
    return b


def host_inputs(d, x0):
    x0g = _shard_tokens_halo(x0, 4)
    x0s = _shard_tokens(x0)
    tabs = [_xpos_tables(c) for c in range(NCORES)]
    rtabs = [_rope_tables(c) for c in range(NCORES)]
    cmasks = _cmask()
    b1v0 = np.asarray(d["L0_ffn_b1"][0]); b1v1 = np.asarray(d["L0_ffn_b1"][1])
    b1_ = np.asarray(d["L1_ffn_b1"])
    b1c_ = np.zeros((128, 32), np.float32)
    for e in range(8):
        for oc in range(4):
            b1c_[:, e * 4 + oc] = b1_[e, oc * 128:(oc + 1) * 128]
    shared = {
        "n1w": _col_pack(d["L0_attn_norm1_w"], 8),
        "n2w": _col_pack(d["L0_attn_norm2_w"], 8),
        "wgt": nbf(np.asarray(d["L0_attn_W_gate"]).T),
        "wit": nbf(np.asarray(d["L0_attn_W_in"]).T),
        "convw": f32(np.asarray(d["L0_attn_conv_w"])[:, 0, :].reshape(12, 128, 4)
                     .transpose(1, 0, 2).reshape(128, 48)),
        "convb": _col_pack(d["L0_attn_conv_b"], 12),
        "rgin_t": nbf(np.asarray(d["L0_attn_rg_in_w"]).T),
        "rggate_t": nbf(np.asarray(d["L0_attn_rg_gate_w"]).T),
        "rginb": _col_pack(d["L0_attn_rg_in_b"], 12),
        "rggateb": _col_pack(d["L0_attn_rg_gate_b"], 12),
        "lam": _col_pack(d["L0_attn_forget_lambda"], 12),
        "wot": nbf(np.asarray(d["L0_attn_W_out"]).T),
        "mgt": nbf(np.asarray(d["L0_attn_mlp_gate_w"]).T),
        "mut": nbf(np.asarray(d["L0_attn_mlp_up_w"]).T),
        "mdt": nbf(np.asarray(d["L0_attn_mlp_down_w"]).T),
        "tAq": None, "wqkv3": nbf(_perm_qk_cols(np.asarray(d["L1_attn_W_qkv"]))),
        "wo3": nbf(np.asarray(d["L1_attn_W_out"])),
        "cT": f32(np.asarray(d["L1_ffn_centroids"]).T),
        "wrT": f32(np.asarray(d["L1_ffn_w_route"]).T),
        "w1e": nbf(np.asarray(d["L1_ffn_w1"]).reshape(8 * D, 512)),
        "b1e": f32(b1c_),
        "w2e": nbf(np.asarray(d["L1_ffn_w2"]).reshape(8 * 512, D)),
        "b2e": nbf(np.asarray(d["L1_ffn_b2"])),
    }
    del shared["tAq"]
    for l in range(2):
        shared[f"wqkv{l}"] = nbf(np.asarray(d["L0_ffn_wqkv"][l]))
        shared[f"qs{l}"] = f32(np.tile(np.asarray(d["L0_ffn_q_scale"][l]), 2)[:, None])
        shared[f"ks{l}"] = f32(np.tile(np.asarray(d["L0_ffn_k_scale"][l]), 2)[:, None])
        shared[f"w1_{l}"] = nbf(np.asarray(d["L0_ffn_w1"][l]))
        b1v = np.asarray(d["L0_ffn_b1"][l])
        shared[f"b1c{l}"] = f32(np.concatenate(
            [_rag_cols(b1v[0:HID2]), _rag_cols(b1v[HID2:])], axis=1))
        shared[f"w2_{l}"] = nbf(np.asarray(d["L0_ffn_w2"][l]))
        shared[f"wo{l}"] = nbf(np.asarray(d["L0_ffn_wo"][l]))
        shared[f"b2c{l}"] = _col_pack(d["L0_ffn_b2"][l], 8)
        shared[f"lg{l}"] = _col_pack(d["L0_ffn_ln_g"][l], 8)
        shared[f"lb{l}"] = _col_pack(d["L0_ffn_ln_b"][l], 8)
    in_maps = []
    for c in range(NCORES):
        k01 = np.ones((128, 21), np.float32)
        if c % 4 == 0:
            k01[:, 0:5] = 0.0
        hm = np.zeros((128, 4), np.float32)
        if c % 4 != 0:
            hm[:, c % 4 - 1] = 1.0
        m = dict(shared)
        m.update({
            "x0g": x0g[c], "x0": x0s[c],
            "tAq": tabs[c][0], "tBq": tabs[c][1],
            "tAk": tabs[c][2], "tBk": tabs[c][3],
            "keep01": f32(k01),
            "gmask": f32(np.full((128, 1), 0.0 if c % 4 == 0 else 1.0)),
            "hmask": f32(hm),
            "cmask": cmasks[c],
            "tA3": rtabs[c][0], "tB3": rtabs[c][1],
        })
        in_maps.append(m)
    return in_maps


def kernel(**inputs):
    x0 = np.asarray(inputs["L0_attn_x"], np.float32)
    in_maps = host_inputs(inputs, x0)
    bld = build_fused()
    res = run_bass_kernel_spmd(bld.nc, in_maps, core_ids=list(range(NCORES)))
    bld.close()
    return _assemble([r["x4"] for r in res.results])
